# version 11
# baseline (speedup 1.0000x reference)
import numpy as np
import concourse.bass as bass
import concourse.mybir as mybir
from concourse.bass_utils import run_bass_kernel_spmd
from concourse.alu_op_type import AluOpType as ALU

F32 = mybir.dt.float32
BF16 = mybir.dt.bfloat16
AF = mybir.ActivationFunctionType

DM, DI = 4096, 8192
NH, HP, NG, DS = 128, 64, 8, 128
CK, CH = 4, 256
EPS = 1e-5
T, TH, HLO = 2048, 1024, 8
THF = TH + HLO
NT3 = THF // 3
NKT = DM // 128
NCC = 18560 // 128
NCH = T // CH

DEBUG_TAPS = []
STOP_AFTER = None
N_CORES = 8
SKIP_CC = False
LAST_TAPS = {}

_ENG = ("sync", "gpsimd", "tensor", "vector", "scalar")
_QUEUE = {"sync": "qs", "gpsimd": "qg", "scalar": "qa"}


class Plan:

    def __init__(self, nc, block):
        self.nc = nc
        self.block = block
        self.instrs = []
        self.writer = {}
        self.readers = {}
        self.sems = {}
        self.emitted = 0
        self.hwm = {}
        self._count = {}
        self._ctx = []

    def make_sems(self):
        keys = ["tensor", "vector", "scalar", "gpsimd", "sync", "cc"]
        for q in ("qs", "qg", "qa"):
            keys += [f"{q}{i}" for i in range(4)]
        for k in keys:
            cm = self.nc.semaphore("sem_" + k)
            self.sems[k] = cm.__enter__()
            self._ctx.append(cm)
        self.qctr = {"qs": 0, "qg": 0, "qa": 0}

    def _add(self, eng, semkey, inc_n, fn, reads, writes, tag):
        idx = len(self.instrs)
        dep = set()
        for r in reads:
            w = self.writer.get(r)
            if w is not None:
                dep.add(w)
        for r in writes:
            w = self.writer.get(r)
            if w is not None:
                dep.add(w)
            for rd in self.readers.get(r, ()):
                dep.add(rd)
        for r in reads:
            self.readers.setdefault(r, []).append(idx)
        for r in writes:
            self.writer[r] = idx
            self.readers[r] = []
        self.instrs.append(dict(idx=idx, eng=eng, semkey=semkey, inc=inc_n, fn=fn,
                                tag=tag, deps=dep, waits=[], cum=None, xwait=None))
        return idx

    def op(self, eng, fn, reads=(), writes=(), inc=True, tag=""):
        return self._add(eng, eng, 1 if inc else 0, fn, reads, writes, tag)

    def dma(self, eng, out, in_, reads=(), writes=(), tag=""):
        fn = lambda e, o=out, i=in_: e.dma_start(out=o, in_=i)
        q = _QUEUE[eng]
        i = self.qctr[q]
        self.qctr[q] += 1
        idx = self._add(eng, f"{q}{i % 4}", 16, fn, reads, writes, tag)
        if i >= 4:
            self.instrs[idx]["xwait"] = (f"{q}{i % 4}", 16 * (i // 4))
        return idx

    def raw(self, eng, semkey, inc_n, fn, reads=(), writes=(), tag=""):
        return self._add(eng, semkey, inc_n, fn, reads, writes, tag)

    def _resolve(self, tgt_idx):
        t = self.instrs[tgt_idx]
        j = tgt_idx
        while j < len(self.instrs):
            ins = self.instrs[j]
            if ins["eng"] == t["eng"] and ins["semkey"] == t["semkey"] and ins["inc"] > 0:
                if ins["cum"] is None:
                    raise RuntimeError(f"dep on unemitted inc: {tgt_idx}:{t['tag']} -> {j}:{ins['tag']}")
                return t["semkey"], ins["cum"]
            j += 1
        raise RuntimeError(f"no covering inc for {tgt_idx}:{t['tag']} ({t['eng']}/{t['semkey']})")

    def emit(self):
        lo, hi = self.emitted, len(self.instrs)
        for i in range(lo, hi):
            ins = self.instrs[i]
            if ins["inc"] > 0:
                self._count[ins["semkey"]] = self._count.get(ins["semkey"], 0) + ins["inc"]
                ins["cum"] = self._count[ins["semkey"]]
        for i in range(lo, hi):
            ins = self.instrs[i]
            if ins["xwait"] is not None:
                ins["waits"].append(ins["xwait"])
            for d in sorted(ins["deps"]):
                td = self.instrs[d]
                if td["semkey"] == ins["eng"]:
                    if td["inc"] > 0 and td["cum"] is not None:
                        ins["waits"].append((td["semkey"], td["cum"]))
                    continue
                ins["waits"].append(self._resolve(d))
        for eng in _ENG:
            mine = [self.instrs[i] for i in range(lo, hi) if self.instrs[i]["eng"] == eng]
            if not mine:
                continue
            dec = getattr(self.block, eng)

            def body(e, mine=mine, eng=eng):
                for ins in mine:
                    for sk, val in ins["waits"]:
                        if self.hwm.get((eng, sk), -1) >= val:
                            continue
                        e.wait_ge(self.sems[sk], val)
                        self.hwm[(eng, sk)] = val
                    r = ins["fn"](e)
                    if ins["inc"] > 0:
                        r.then_inc(self.sems[ins["semkey"]], ins["inc"])
            dec(body)
        self.emitted = hi

    def barrier(self, tag="bar"):
        lasts = {}
        for ins in self.instrs:
            if ins["inc"] > 0:
                lasts[(ins["eng"], ins["semkey"])] = ins["idx"]
        dep_idxs = set(lasts.values())
        for eng in _ENG:
            i = self._add(eng, eng, 1, lambda e: e.nop(), (), (), f"{tag} {eng}")
            self.instrs[i]["deps"] |= dep_idxs

    def final_counts(self):
        return dict(self._count)


def _stage_host(inputs):
    g = lambda k: np.ascontiguousarray(np.asarray(inputs[k], dtype=np.float32))
    x = g("L0_attn_hidden_states")
    w_in = g("L0_attn_in_proj_w")
    conv_w = g("L0_attn_conv_w")
    conv_b = g("L0_attn_conv_b")
    dt_bias = g("L0_attn_dt_bias")
    A_log = g("L0_attn_A_log")
    D = g("L0_attn_D")
    norm_w = g("L0_attn_norm_w")
    w_out = g("L0_attn_out_proj_w")
    W_base = g("L0_ffn_W_base")
    b_base = g("L0_ffn_b_base")
    W_router = g("L0_ffn_W_router")
    A_stack = g("L0_ffn_A_stack")
    B_stack = g("L0_ffn_B_stack")

    sh = {}
    w_inT = np.ascontiguousarray(w_in.T)
    sh["w_in_t"] = np.ascontiguousarray(
        w_inT.reshape(NKT, 128, NCC, 128).transpose(2, 1, 0, 3)).reshape(NCC, 128, NKT * 128)
    w_outT = np.ascontiguousarray(w_out.T)
    sh["w_out_t"] = np.ascontiguousarray(
        w_outT.reshape(64, 128, 32, 128).transpose(2, 1, 0, 3)).reshape(32, 128, 64 * 128)
    w_baseT = np.ascontiguousarray(W_base.T)
    sh["w_base_t"] = np.ascontiguousarray(
        w_baseT.reshape(32, 128, 32, 128).transpose(2, 1, 0, 3)).reshape(32, 128, 32 * 128)
    sh["w_router_t"] = np.ascontiguousarray(
        W_router.T.reshape(32, 128, 8).transpose(1, 0, 2)).reshape(128, 256)
    sh["a_all_t"] = np.ascontiguousarray(
        A_stack.transpose(2, 0, 1).reshape(4096, 128).reshape(32, 128, 128)
        .transpose(1, 0, 2)).reshape(128, 4096)
    sh["b_flat"] = np.ascontiguousarray(B_stack.transpose(0, 2, 1)).reshape(128, 4096)
    sh["conv_w4"] = np.ascontiguousarray(conv_w.reshape(80, 128, 4).transpose(1, 0, 2)).reshape(128, 320)
    sh["conv_b80"] = np.ascontiguousarray(conv_b.reshape(80, 128).T)
    sh["dt_bias_c"] = dt_bias.reshape(128, 1)
    sh["a_log_c"] = A_log.reshape(128, 1)
    sh["d_pair"] = np.ascontiguousarray(
        np.concatenate([np.tile(D[0::2], (64, 1)), np.tile(D[1::2], (64, 1))], axis=0))
    sh["norm_pair"] = np.ascontiguousarray(norm_w.reshape(64, 128).T)
    sh["b_base32"] = np.ascontiguousarray(b_base.reshape(32, 128).T)
    sh["identity"] = np.eye(128, dtype=np.float32)
    sh["dmask"] = np.triu(np.ones((128, 128), dtype=np.float32))
    e8 = np.zeros((8, 128), np.float32)
    for e in range(8):
        e8[e, e * 16:(e + 1) * 16] = 1.0
    sh["e8"] = e8

    per = []
    for c in range(8):
        s, h = c // 2, c % 2
        xsT = x[s].T
        if h == 0:
            xT = np.concatenate([np.zeros((4096, HLO), np.float32), xsT[:, :T]], axis=1)
        else:
            xT = np.ascontiguousarray(xsT[:, T - HLO:])
        msel = np.zeros((128, 8), np.float32)
        if h == 1:
            msel[:, c - 1] = 1.0
        per.append(dict(xT=np.ascontiguousarray(xT), msel=msel))
    return sh, per


class MK:
    def __init__(self, nc, taps):
        self.nc = nc
        self.taps = taps
        self.tap_jobs = []
        ap = {}
        ap["xT"] = nc.dram_tensor("xT", [4096, T + HLO], F32, kind="ExternalInput").ap()
        for n, shp in [("msel", [128, 8]), ("w_router_t", [128, 256]),
                       ("a_all_t", [128, 4096]), ("b_flat", [128, 4096]),
                       ("conv_w4", [128, 320]), ("conv_b80", [128, 80]),
                       ("dt_bias_c", [128, 1]), ("a_log_c", [128, 1]),
                       ("d_pair", [128, 64]), ("norm_pair", [128, 64]), ("b_base32", [128, 32]),
                       ("identity", [128, 128]), ("dmask", [128, 128]), ("e8", [8, 128]),
                       ("w_in_t", [NCC, 128, NKT * 128]), ("w_out_t", [32, 128, 64 * 128]),
                       ("w_base_t", [32, 128, 32 * 128])]:
            ap[n] = nc.dram_tensor(n, shp, F32, kind="ExternalInput").ap()
        self.ap = ap
        self.out_t = nc.dram_tensor("out_t", [4096, T], F32, kind="ExternalOutput").ap()
        sc = {}
        sc["B"] = nc.dram_tensor("sc_B", [NG * 128, T], BF16).ap()
        sc["C"] = nc.dram_tensor("sc_C", [NG * 128, T], BF16).ap()
        sc["BT"] = nc.dram_tensor("sc_BT", [16, NG, 128, 128], BF16).ap()
        sc["xs"] = nc.dram_tensor("sc_xs", [DI, T], BF16).ap()
        sc["z"] = nc.dram_tensor("sc_z", [DI, T], BF16).ap()
        sc["xdt"] = nc.dram_tensor("sc_xdt", [16, 64, 128, 128], BF16).ap()
        sc["S"] = nc.dram_tensor("sc_S", [NCH, 128, DI], F32).ap()
        sc["yp"] = nc.dram_tensor("sc_yp", [DI, T], BF16).ap()
        sc["mo"] = nc.dram_tensor("sc_mo", [4096, T], F32).ap()
        sc["x2"] = nc.dram_tensor("sc_x2", [4096, T], F32).ap()
        sc["ffn"] = nc.dram_tensor("sc_ffn", [4096, T], F32).ap()
        sc["wob"] = nc.dram_tensor("sc_wob", [32, 128, 64 * 128], BF16).ap()
        sc["wbb"] = nc.dram_tensor("sc_wbb", [32, 128, 32 * 128], BF16).ap()
        self.sc = sc
        self.cc_in = nc.dram_tensor("cc_in", [128, DI], F32).ap()
        self.cc_out = nc.dram_tensor("cc_out", [8 * 128, DI], F32).ap()
        self.wctr = 0

    def tap_sc(self, name):
        if name in self.taps and name in self.sc:
            src = self.sc[name]
            dst = self.nc.dram_tensor("tap_" + name, list(src.shape), src.dtype,
                                      kind="ExternalOutput").ap()
            self.tap_jobs.append((name, dst, src))

    def tap_sb(self, name, src_ap, reads=()):
        if name in self.taps:
            dst = self.nc.dram_tensor("tap_" + name, list(src_ap.shape), src_ap.dtype,
                                      kind="ExternalOutput").ap()
            self.P.dma("sync", dst, src_ap, reads=reads, writes=(), tag=f"tap {name}")


    def phase0(self):
        P, cst = self.P, self.cst
        for n in cst:
            P.dma("sync", cst[n][:], self.ap[n], writes=(f"c_{n}",), tag=f"ld_{n}")
        P.op("vector", lambda e: e.memset(self.onesc[:], 1.0), writes=("ones",))
        P.op("vector", lambda e: e.memset(self.zeroc[:], 0.0), writes=("zeroc",))
        P.op("vector", lambda e: e.memset(self.onesr[0:1, :], 1.0), writes=("onesr",))
        P.op("vector", lambda e: e.memset(self.onesb[:], 1.0), writes=("onesb",))
        P.op("vector", lambda e: e.memset(self.epsc[:], EPS), writes=("ceps",))
        P.op("vector", lambda e: e.memset(self.S[:], 0.0), writes=("S",))
        P.op("scalar", lambda e: e.copy(self.idb[:], cst["identity"][:]),
             reads=("c_identity",), writes=("idb",))
        P.op("scalar", lambda e: e.activation(self.a_neg[:], cst["a_log_c"][:], AF.Exp),
             reads=("c_a_log_c",), writes=("aneg",))
        P.op("vector", lambda e: e.tensor_scalar(self.a_neg[:], self.a_neg[:], -1.0, None, ALU.mult),
             reads=("aneg",), writes=("aneg",))
        P.op("vector", lambda e: e.memset(self.dac[:, 0:128], 1.0), writes=("dac0",))

    def inproj_cc(self, cc, L, tag, halo=True):
        P = self.P
        slot = self.wctr % 2
        self.wctr += 1
        wbuf, xln, pb = L["wbuf"], L["xln"], self.pb
        W = NKT * 128 * slot
        P.dma("gpsimd", wbuf[:, W:W + NKT * 128], self.ap["w_in_t"][cc],
              reads=(), writes=(f"wb{slot}",), tag=f"{tag} wld")
        if halo:
            tiles = [(nt, nt * NT3, NT3) for nt in range(3)]
        else:
            tiles = [(nt, HLO + nt * 512, 512) for nt in range(2)]
        for nt, off, nn in tiles:
            for kt in range(NKT):
                last = kt == NKT - 1
                P.op("tensor", lambda e, nt=nt, kt=kt, W=W, off=off, nn=nn: e.matmul(
                    pb[nt][:, 0:nn], wbuf[:, W + kt * 128:W + (kt + 1) * 128],
                    xln[:, kt * THF + off:kt * THF + off + nn],
                    start=(kt == 0), stop=last, skip_group_check=True),
                    reads=(f"wb{slot}", f"xln{kt}"),
                    writes=((f"pb{nt}",) if (kt == 0 or last) else ()),
                    inc=last, tag=f"{tag} mm{nt}.{kt}")

    def ln1(self, th, L):
        P, ap, cst = self.P, self.ap, self.cst
        xtf, xln, scr2, dtb = L["xtf"], L["xln"], L["scr2"], L["dtb"]
        pb = self.pb
        m1, rstd1, onesc, onesr = self.m1, self.rstd1, self.onesc, self.onesr
        c0 = th * TH
        sA = lambda: scr2[:, 0:THF]
        sB = lambda: scr2[:, THF:2 * THF]
        for kt in range(NKT):
            P.dma("sync", xtf[:], ap["xT"][kt * 128:(kt + 1) * 128, c0:c0 + THF],
                  writes=("xtf0", "raw"), tag=f"ln1.{th} ld{kt}")
            for nt in range(3):
                fl = kt == 0
                la = kt == NKT - 1
                P.op("tensor", lambda e, nt=nt, fl=fl, la=la: e.matmul(
                    pb[nt][0:1, 0:NT3], onesc[:], xtf[:, nt * NT3:(nt + 1) * NT3],
                    start=fl, stop=la, skip_group_check=True),
                    reads=("xtf0", "ones"), writes=((f"pb{nt}",) if (fl or la) else ()),
                    inc=True, tag=f"ln1.{th} sum{kt}.{nt}")
            sq = sA() if kt % 2 == 0 else sB()
            sqr = "scrA" if kt % 2 == 0 else "scrB"
            P.op("scalar", lambda e, sq=sq: e.activation(sq, xtf[:], AF.Square),
                 reads=("xtf0",), writes=(sqr,), tag=f"ln1.{th} sq{kt}")
            P.op("vector", lambda e, kt=kt: e.tensor_copy(
                xln[:, kt * THF:(kt + 1) * THF], xtf[:]),
                reads=("xtf0",), writes=(f"xln{kt}",), tag=f"ln1.{th} raw{kt}")
            for nt in range(3):
                fl = kt == 0
                la = kt == NKT - 1
                P.op("tensor", lambda e, nt=nt, fl=fl, la=la, sq=sq: e.matmul(
                    pb[3 + nt][0:1, 0:NT3], onesc[:], sq[:, nt * NT3:(nt + 1) * NT3],
                    start=fl, stop=la, skip_group_check=True),
                    reads=(sqr, "ones"), writes=((f"pb{3+nt}",) if (fl or la) else ()),
                    inc=True, tag=f"ln1.{th} ssq{kt}.{nt}")
        if STOP_AFTER == "ln1s":
            return
        mo = th * THF
        r1 = dtb[0:1, THF:2 * THF]
        r2 = dtb[0:1, 2 * THF:3 * THF]
        for nt in range(3):
            P.op("vector", lambda e, nt=nt: e.tensor_scalar(
                m1[0:1, mo + nt * NT3:mo + (nt + 1) * NT3], pb[nt][0:1, 0:NT3],
                1.0 / DM, None, ALU.mult),
                reads=(f"pb{nt}",), writes=("m1",), tag=f"ln1.{th} mean{nt}")
            P.op("vector", lambda e, nt=nt: e.tensor_scalar(
                r1[0:1, nt * NT3:(nt + 1) * NT3], pb[3 + nt][0:1, 0:NT3],
                1.0 / DM, None, ALU.mult),
                reads=(f"pb{3+nt}",), writes=("dt1",), tag=f"ln1.{th} msq{nt}")
        P.op("vector", lambda e: e.tensor_tensor(r2, m1[0:1, mo:mo + THF], m1[0:1, mo:mo + THF],
                                                 ALU.mult),
             reads=("m1",), writes=("dt2",), tag=f"ln1.{th} m^2")
        P.op("vector", lambda e: e.tensor_tensor(r1, r1, r2, ALU.subtract),
             reads=("dt1", "dt2"), writes=("dt1",), tag=f"ln1.{th} var")
        P.op("scalar", lambda e: e.activation(r2, r1, AF.Ln, bias=self.epsc[0:1, 0:1]),
             reads=("dt1", "ceps"), writes=("dt2",), tag=f"ln1.{th} ln")
        P.op("scalar", lambda e: e.activation(rstd1[0:1, mo:mo + THF], r2, AF.Exp, scale=-0.5),
             reads=("dt2",), writes=("rstd1",), tag=f"ln1.{th} rstd")
        if STOP_AFTER == "ln1r":
            return
        for nt in range(3):
            bk = nt % 2 * 6
            P.op("tensor", lambda e, nt=nt, bk=bk: e.matmul(
                pb[bk][:, 0:NT3], onesr[0:1, 0:128],
                m1[0:1, mo + nt * NT3:mo + (nt + 1) * NT3], start=True, stop=True),
                reads=("m1", "onesr"), writes=(f"pb{bk}",), tag=f"ln1.{th} mb{nt}")
            P.op("vector", lambda e, nt=nt, bk=bk: e.tensor_copy(scr2[:, nt * NT3:(nt + 1) * NT3],
                                                                 pb[bk][:, 0:NT3]),
                 reads=(f"pb{bk}",), writes=("scrA",), tag=f"ln1.{th} mbe{nt}")
            bk2 = (nt + 1) % 2 * 6
            P.op("tensor", lambda e, nt=nt, bk2=bk2: e.matmul(
                pb[bk2][:, 0:NT3], onesr[0:1, 0:128],
                rstd1[0:1, mo + nt * NT3:mo + (nt + 1) * NT3], start=True, stop=True),
                reads=("rstd1", "onesr"), writes=(f"pb{bk2}",), tag=f"ln1.{th} rb{nt}")
            P.op("vector", lambda e, nt=nt, bk2=bk2: e.tensor_copy(
                scr2[:, THF + nt * NT3:THF + (nt + 1) * NT3], pb[bk2][:, 0:NT3]),
                 reads=(f"pb{bk2}",), writes=("scrB",), tag=f"ln1.{th} rbe{nt}")
        if STOP_AFTER == "ln1c":
            return
        for kt in range(NKT):
            P.op("vector", lambda e, kt=kt: e.tensor_tensor(
                xln[:, kt * THF:(kt + 1) * THF], xln[:, kt * THF:(kt + 1) * THF],
                scr2[:, 0:THF], ALU.subtract),
                reads=(f"xln{kt}", "scrA"), writes=(f"xln{kt}",), tag=f"ln1b.{th} sub{kt}")
            P.op("vector", lambda e, kt=kt: e.tensor_tensor(
                xln[:, kt * THF:(kt + 1) * THF], xln[:, kt * THF:(kt + 1) * THF],
                scr2[:, THF:2 * THF], ALU.mult),
                reads=(f"xln{kt}", "scrB"), writes=(f"xln{kt}",), tag=f"ln1b.{th} mul{kt}")

    def conv_silu(self, cbc, L, tag, out_bf, out_res):
        P, cst = self.P, self.cst
        rawb, scr2, pb = L["rawb"], L["scr2"], self.pb
        for nt in range(3):
            P.op("scalar", lambda e, nt=nt: e.copy(rawb[:, nt * NT3:(nt + 1) * NT3],
                                                   pb[nt][:, 0:NT3]),
                 reads=(f"pb{nt}",), writes=("raw", "xtf0"), tag=f"{tag} ev{nt}")
        cvB = L["cvb"]
        P.op("vector", lambda e: e.tensor_scalar(scr2[:, 0:TH], rawb[:, 5:5 + TH],
                                                 cst["conv_w4"][:, cbc * 4:cbc * 4 + 1],
                                                 None, ALU.mult),
             reads=("raw", "c_conv_w4"), writes=("scrA",), tag=f"{tag} tap0")
        P.op("vector", lambda e: e.scalar_tensor_tensor(
            scr2[:, 0:TH], rawb[:, 6:6 + TH],
            cst["conv_w4"][:, cbc * 4 + 1:cbc * 4 + 2],
            scr2[:, 0:TH], ALU.mult, ALU.add),
            reads=("raw", "scrA", "c_conv_w4"), writes=("scrA",), tag=f"{tag} tap1")
        cvC = L["cvc"]
        w2b = cst["conv_w4"][:, cbc * 4 + 2:cbc * 4 + 3].broadcast_to([128, 1, TH])
        w3b = cst["conv_w4"][:, cbc * 4 + 3:cbc * 4 + 4].broadcast_to([128, 1, TH])
        P.op("gpsimd", lambda e, w3b=w3b: e.tensor_tensor(
            cvC[:], rawb[:, 8:8 + TH], w3b, ALU.mult),
            reads=("raw", "c_conv_w4"), writes=("cvC", "dt1"), tag=f"{tag} tap3")
        P.op("gpsimd", lambda e, w2b=w2b: e.tensor_tensor(
            cvB[:], rawb[:, 7:7 + TH], w2b, ALU.mult),
            reads=("raw", "c_conv_w4"), writes=("cvB", "dt0"), tag=f"{tag} tap2")
        P.op("gpsimd", lambda e: e.tensor_tensor(cvB[:], cvB[:], cvC[:], ALU.add),
             reads=("cvB", "cvC"), writes=("cvB", "dt0"), tag=f"{tag} tap23")
        P.op("gpsimd", lambda e: e.tensor_tensor(scr2[:, THF:THF + TH], scr2[:, 0:TH],
                                                 cvB[:], ALU.add),
             reads=("scrA", "cvB"), writes=("scrB",), tag=f"{tag} tapm")
        P.op("scalar", lambda e: e.activation(out_bf, scr2[:, THF:THF + TH], AF.Silu,
                                              bias=cst["conv_b80"][:, cbc:cbc + 1]),
             reads=("scrB", "c_conv_b80"), writes=(out_res,), tag=f"{tag} silu")

    def dt_phase(self, th, L):
        P, cst = self.P, self.cst
        dtb, dtT, ddT = L["dtb"], L["dtT"], L["ddT"]
        od, a_cs, dac, odlbc = self.od, self.a_cs, self.dac, self.odlbc
        pb = self.pb
        idf = cst["identity"]
        onesr = self.onesr
        B = THF
        s0 = dtb[:, 0:THF]
        s1 = dtb[:, B:2 * B]
        s2 = dtb[:, 2 * B:3 * B]
        self.inproj_cc(144, L, f"dt{th}", halo=False)
        for nt in range(2):
            P.op("vector", lambda e, nt=nt: e.tensor_copy(s0[:, nt * 512:(nt + 1) * 512],
                                                          pb[nt][:, 0:512]),
                 reads=(f"pb{nt}",), writes=("dt0",), tag=f"dt{th} ev{nt}")
        P.op("vector", lambda e: e.tensor_scalar(s1[:, 0:TH], s0[:, 0:TH], cst["dt_bias_c"][:], None, ALU.add),
             reads=("dt0", "c_dt_bias_c"), writes=("dt1",), tag=f"dt{th} bias")
        P.op("scalar", lambda e: e.activation(s2[:, 0:TH], s1[:, 0:TH], AF.Abs),
             reads=("dt1",), writes=("dt2",), tag=f"dt{th} abs")
        P.op("scalar", lambda e: e.activation(s2[:, 0:TH], s2[:, 0:TH], AF.Exp, scale=-1.0),
             reads=("dt2",), writes=("dt2",), tag=f"dt{th} exp")
        P.op("scalar", lambda e: e.activation(s2[:, 0:TH], s2[:, 0:TH], AF.Ln, bias=1.0),
             reads=("dt2",), writes=("dt2",), tag=f"dt{th} ln1p")
        P.op("vector", lambda e: e.tensor_scalar(s1[:, 0:TH], s1[:, 0:TH], 0.0, None, ALU.max),
             reads=("dt1",), writes=("dt1",), tag=f"dt{th} relu")
        P.op("vector", lambda e: e.tensor_tensor(s0[:, 0:TH], s1[:, 0:TH], s2[:, 0:TH], ALU.add),
             reads=("dt1", "dt2"), writes=("dt0",), tag=f"dt{th} dt")
        P.op("vector", lambda e: e.tensor_scalar(s1[:, 0:TH], s0[:, 0:TH],
                                                 self.a_neg[:], None, ALU.mult),
             reads=("dt0", "aneg"), writes=("dt1",), tag=f"dt{th} dA")
        t0 = th * TH
        for c in range(4):
            P.op("vector", lambda e, c=c: e.tensor_tensor_scan(
                a_cs[:, t0 + c * CH:t0 + (c + 1) * CH], s1[:, c * CH:(c + 1) * CH],
                s1[:, c * CH:(c + 1) * CH], 0.0, ALU.add, ALU.bypass),
                reads=("dt1",), writes=("acs",), tag=f"dt{th} cs{c}")
        P.op("scalar", lambda e: e.activation(od[:, t0:t0 + TH], a_cs[:, t0:t0 + TH], AF.Exp),
             reads=("acs",), writes=("od",), tag=f"dt{th} od")
        for c in range(4):
            P.op("vector", lambda e, c=c: e.tensor_scalar(
                s1[:, c * CH:(c + 1) * CH], a_cs[:, t0 + c * CH:t0 + (c + 1) * CH],
                a_cs[:, t0 + (c + 1) * CH - 1:t0 + (c + 1) * CH], None, ALU.subtract),
                reads=("acs",), writes=("dt1",), tag=f"dt{th} dec{c}a")
        P.op("scalar", lambda e: e.activation(s1[:, 0:TH], s1[:, 0:TH], AF.Exp, scale=-1.0),
             reads=("dt1",), writes=("dt1",), tag=f"dt{th} dec")
        P.op("vector", lambda e: e.tensor_tensor(s2[:, 0:TH], s0[:, 0:TH],
                                                 s1[:, 0:TH], ALU.mult),
             reads=("dt0", "dt1"), writes=("dt2",), tag=f"dt{th} dtdec")
        for st in range(8):
            dst = slice(st * 128, (st + 1) * 128)
            P.op("tensor", lambda e, st=st: e.transpose(
                pb[3][:, 0:128], s0[:, st * 128:(st + 1) * 128], idf[:]),
                reads=("dt0", "c_identity"), writes=("pb3",), tag=f"dt{th} trT{st}")
            P.op("vector", lambda e, dst=dst: e.tensor_copy(dtT[:, dst], pb[3][:, 0:128]),
                 reads=("pb3",), writes=("dtT",), tag=f"dt{th} trTe{st}")
            P.op("tensor", lambda e, st=st: e.transpose(
                pb[4][:, 0:128], s2[:, st * 128:(st + 1) * 128], idf[:]),
                reads=("dt2", "c_identity"), writes=("pb4",), tag=f"dt{th} trD{st}")
            P.op("vector", lambda e, dst=dst: e.tensor_copy(ddT[:, dst], pb[4][:, 0:128]),
                 reads=("pb4",), writes=("ddT",), tag=f"dt{th} trDe{st}")
        for c in range(4):
            gc = th * 4 + c
            P.dma("sync", self.flatr[0:1, 0:128],
                  a_cs[:, t0 + (c + 1) * CH - 1:t0 + (c + 1) * CH],
                  reads=("acs",), writes=("odlrow",), tag=f"dt{th} odl{c}")
            P.op("tensor", lambda e: e.matmul(pb[3][:, 0:128], onesr[0:1, 0:128],
                                              self.flatr[0:1, 0:128], start=True, stop=True),
                 reads=("odlrow", "onesr"), writes=("pb3",), tag=f"dt{th} odlb{c}")
            P.op("scalar", lambda e, gc=gc: e.activation(odlbc[:, gc * 128:(gc + 1) * 128],
                                                         pb[3][:, 0:128], AF.Exp),
                 reads=("pb3",), writes=(f"odlbc{gc}",), tag=f"dt{th} odlbe{c}")
            if gc + 1 < NCH:
                P.op("vector", lambda e, gc=gc: e.tensor_tensor(
                    dac[:, (gc + 1) * 128:(gc + 2) * 128], dac[:, gc * 128:(gc + 1) * 128],
                    odlbc[:, gc * 128:(gc + 1) * 128], ALU.mult),
                    reads=(f"odlbc{gc}", f"dac{gc}"), writes=(f"dac{gc+1}",), tag=f"dt{th} dac{gc}")

    def group_phase(self, th, g, L):
        P, cst, sc = self.P, self.cst, self.sc
        pb, pbT = self.pb, self.pbT
        btst = L["btst"]
        dtT, ddT = L["dtT"], L["ddT"]
        S, odlbc, gtmp = self.S, self.odlbc, L["gtmp"]
        idb = self.idb
        t0 = th * TH

        def conv_bc(cc, cbc, dest_rows, dest, res, bt_too, tag):
            bcbh = L["bcb"][:, 0:TH]
            self.inproj_cc(cc, L, tag)
            self.conv_silu(cbc, L, tag, bcbh, "bcb0")
            P.dma("sync", dest[dest_rows:dest_rows + 128, t0:t0 + TH], bcbh,
                  reads=("bcb0",), writes=(res,), tag=f"{tag} st")
            if bt_too:
                for st in range(8):
                    P.op("tensor", lambda e, st=st, bcbh=bcbh: e.transpose(
                        pbT[:, st * 128:(st + 1) * 128], bcbh[:, st * 128:(st + 1) * 128], idb[:]),
                        reads=("bcb0", "idb"), writes=(("pbT",) if st in (0, 7) else ()),
                        inc=(st == 7), tag=f"{tag} btT{st}")
                P.op("vector", lambda e: e.tensor_copy(btst[:], pbT[:]),
                     reads=("pbT",), writes=("btst",), tag=f"{tag} btTe")
                P.dma("scalar", sc["BT"][th * 8:th * 8 + 8, g].rearrange("st p t -> p st t"),
                      btst[:].rearrange("p (st t) -> p st t", st=8),
                      reads=("btst",), writes=("scBT",), tag=f"{tag} btst")

        conv_bc(128 + g, 64 + g, g * 128, sc["B"], "scB", True, f"B{th}.{g}")
        conv_bc(136 + g, 72 + g, g * 128, sc["C"], "scC", False, f"C{th}.{g}")

        bcb2, xst, gxsT, gxdd = L["bcb"], L["xst"], L["gxsT"], L["gxdd"]

        def stageMV(j):
            par = j % 2
            cc = 64 + g * 8 + j
            cbc = g * 8 + j
            tag = f"x{th}.{g}.{j}"
            self.inproj_cc(cc, L, tag)
            self.conv_silu(cbc, L, tag, bcb2[:, par * TH:(par + 1) * TH], f"bcb{par}")
            P.dma("sync", sc["xs"][cbc * 128:(cbc + 1) * 128, t0:t0 + TH],
                  bcb2[:, par * TH:(par + 1) * TH],
                  reads=(f"bcb{par}",), writes=("scxs",), tag=f"{tag} st")

        def stageTDpe(j):
            par = j % 2
            tag = f"x{th}.{g}.{j}"
            for st in range(8):
                P.op("tensor", lambda e, st=st, par=par: e.transpose(
                    pbT[:, st * 128:(st + 1) * 128],
                    bcb2[:, par * TH + st * 128:par * TH + (st + 1) * 128], idb[:]),
                    reads=(f"bcb{par}", "idb"), writes=(("pbT",) if st in (0, 7) else ()),
                    inc=(st == 7), tag=f"{tag} xT{st}")

        def stageTDdve(j):
            par = j % 2
            cbc = g * 8 + j
            tag = f"x{th}.{g}.{j}"
            P.op("vector", lambda e, par=par: e.tensor_copy(
                gxsT[:, par * 1024:(par + 1) * 1024], pbT[:]),
                reads=("pbT",), writes=(f"gxsT{par}",), tag=f"{tag} xTe")
            xin = gxsT[:, par * 1024:(par + 1) * 1024].rearrange(
                "p (st hh pp) -> p st hh pp", st=8, hh=2)
            dAP = dtT[:, :].rearrange("p (st r) -> p st r", st=8)[:, :, 2 * cbc:2 * cbc + 2] \
                .broadcast_to([128, 8, 2, 64])
            ddAP = ddT[:, :].rearrange("p (st r) -> p st r", st=8)[:, :, 2 * cbc:2 * cbc + 2] \
                .broadcast_to([128, 8, 2, 64])
            P.op("vector", lambda e, xin=xin, dAP=dAP, par=par: e.tensor_tensor(
                xst[:, par * 1024:(par + 1) * 1024].rearrange(
                    "p (st hh pp) -> p st hh pp", st=8, hh=2),
                xin, dAP, ALU.mult),
                reads=(f"gxsT{par}", "dtT"), writes=(f"xst{par}",), tag=f"{tag} xdt")
            P.op("vector", lambda e, xin=xin, ddAP=ddAP, par=par: e.tensor_tensor(
                gxdd[:, par * 1024:(par + 1) * 1024].rearrange(
                    "p (st hh pp) -> p st hh pp", st=8, hh=2),
                xin, ddAP, ALU.mult),
                reads=(f"gxsT{par}", "ddT"), writes=(f"gxdd{par}",), tag=f"{tag} xdd")
            P.dma("scalar", sc["xdt"][th * 8:th * 8 + 8, cbc].rearrange("st p t -> p st t"),
                  xst[:, par * 1024:(par + 1) * 1024].rearrange("p (st t) -> p st t", st=8),
                  reads=(f"xst{par}",), writes=("scxdt",), tag=f"{tag} xdtst")

        sstS = L["sstS"]

        def stageST(j):
            par = j % 2
            cbc = g * 8 + j
            so = 0
            tag = f"x{th}.{g}.{j}"
            for c in range(4):
                gc = th * 4 + c
                csl = slice(cbc * 128, (cbc + 1) * 128)
                P.op("vector", lambda e, c=c, csl=csl, so=so: e.tensor_copy(
                    sstS[:, so + c * 128:so + (c + 1) * 128], S[:, csl]),
                    reads=("S",), writes=("sstS",), tag=f"{tag} Scp{c}")
                bank = 5 if c % 2 == 0 else 6
                n_mm = 0
                for sti in range(2):
                    st = 2 * c + sti
                    for hh in range(2):
                        first = n_mm == 0
                        last = n_mm == 3
                        P.op("tensor", lambda e, st=st, hh=hh, bank=bank, first=first,
                             last=last, par=par: e.matmul(
                            pb[bank][:, hh * 64:hh * 64 + 64],
                            btst[:, st * 128:(st + 1) * 128],
                            gxdd[:, par * 1024 + st * 128 + hh * 64:par * 1024 + st * 128 + hh * 64 + 64],
                            start=first, stop=last, skip_group_check=True),
                            reads=("btst", f"gxdd{par}"),
                            writes=((f"pb{bank}",) if (first or last) else ()),
                            inc=last, tag=f"{tag} Smm{c}.{n_mm}")
                        n_mm += 1
                obs = odlbc[:, gc * 128 + 2 * cbc:gc * 128 + 2 * cbc + 2]
                P.op("vector", lambda e, csl=csl, obs=obs: e.tensor_tensor(
                    gtmp[:, 0:128], S[:, csl], obs.broadcast_to([128, 2, 64]), ALU.mult),
                    reads=("S", f"odlbc{gc}"), writes=("gtmp",), tag=f"{tag} Sdec{c}")
                P.op("vector", lambda e, csl=csl, bank=bank: e.tensor_tensor(
                    S[:, csl], gtmp[:, 0:128], pb[bank][:, 0:128], ALU.add),
                    reads=("gtmp", f"pb{bank}"), writes=("S",), tag=f"{tag} Supd{c}")
            csl = slice(cbc * 128, (cbc + 1) * 128)
            P.dma("scalar", sc["S"][th * 4:th * 4 + 4, :, csl].rearrange("c p t -> p c t"),
                  sstS[:, so:so + 512].rearrange("p (c t) -> p c t", c=4),
                  reads=("sstS",), writes=("scS",), tag=f"{tag} Sst")

        for j in range(8):
            if j >= 2:
                stageTDdve(j - 2)
                stageST(j - 2)
            stageMV(j)
            if j >= 1:
                stageTDpe(j - 1)
        stageTDdve(6)
        stageST(6)
        stageTDpe(7)
        stageTDdve(7)
        stageST(7)

    def z_phase(self, th, L):
        P, sc = self.P, self.sc
        pb = self.pb
        t0 = th * TH
        for cc in range(64):
            par = cc % 2
            zb = L["bcb"][:, par * TH:(par + 1) * TH]
            tag = f"z{th}.{cc}"
            self.inproj_cc(cc, L, tag, halo=False)
            for nt in range(2):
                P.op("scalar", lambda e, nt=nt, zb=zb: e.activation(zb[:, nt * 512:(nt + 1) * 512],
                                                                    pb[nt][:, 0:512], AF.Silu),
                     reads=(f"pb{nt}",), writes=(f"bcb{par}",), tag=f"{tag} ev{nt}")
            P.dma("sync", sc["z"][cc * 128:(cc + 1) * 128, t0:t0 + TH], zb,
                  reads=(f"bcb{par}",), writes=("scz",), tag=f"{tag} st")

    def subpass1(self):
        nc, P = self.nc, self.P
        with nc.sbuf_tensor("xln", [128, NKT * THF], BF16) as xln, \
             nc.sbuf_tensor("scr2", [128, 2 * THF], F32) as scr2, \
             nc.sbuf_tensor("wbuf", [128, 2 * NKT * 128], BF16) as wbuf, \
             nc.sbuf_tensor("rawb", [128, THF], F32) as rawb, \
             nc.sbuf_tensor("bcb", [128, 2 * TH], BF16) as bcb, \
             nc.sbuf_tensor("xst", [128, 2048], BF16) as xst, \
             nc.sbuf_tensor("btst", [128, 8 * 128], BF16) as btst, \
             nc.sbuf_tensor("gxsT", [128, 2048], BF16) as gxsT, \
             nc.sbuf_tensor("gxdd", [128, 2048], BF16) as gxdd, \
             nc.sbuf_tensor("dtb", [128, 3 * THF], F32) as dtb, \
             nc.sbuf_tensor("dtT", [128, TH], F32) as dtT, \
             nc.sbuf_tensor("ddT", [128, TH], F32) as ddT, \
             nc.sbuf_tensor("gtmp", [128, 256], F32) as gtmp, \
             nc.sbuf_tensor("sstS", [128, 512], F32) as sstS:
            xtf = rawb
            cvb = dtb[:, 0:TH]
            cvc = dtb[:, THF:THF + TH]
            L = dict(xln=xln, xtf=xtf, scr2=scr2, wbuf=wbuf, rawb=rawb, bcb=bcb,
                     xst=xst, btst=btst, gxsT=gxsT, gxdd=gxdd, dtb=dtb, dtT=dtT, ddT=ddT,
                     gtmp=gtmp, cvb=cvb, cvc=cvc, sstS=sstS)
            def stop(name):
                return STOP_AFTER == name
            done = False
            for th in range(2):
                self.ln1(th, L)
                if stop(f"ln1.{th}") or STOP_AFTER in ("ln1s", "ln1r", "ln1c"):
                    done = True; break
                self.dt_phase(th, L)
                if stop(f"dt.{th}"):
                    done = True; break
                for g in range(NG):
                    self.group_phase(th, g, L)
                    if stop(f"g{g}.{th}"):
                        done = True; break
                if done:
                    break
                self.z_phase(th, L)
                if stop(f"sub1.{th}"):
                    done = True; break
            self.tap_sb("xln", xln[:], tuple(f"xln{k}" for k in range(NKT)))
            self.tap_sb("dtT", dtT[:], ("dtT",))
            self.tap_sb("ddT", ddT[:], ("ddT",))
            self.tap_sb("acs", self.a_cs[:], ("acs",))
            self.tap_sb("od", self.od[:], ("od",))
            self.tap_sb("Sfin", self.S[:], ("S",))
            self.tap_sb("dac", self.dac[:], tuple(f"dac{i}" for i in range(NCH)))
            self.tap_sb("odlbc", self.odlbc[:], tuple(f"odlbc{i}" for i in range(NCH)))
            self.tap_sb("m1", self.m1[:], ("m1",))
            self.tap_sb("rstd1", self.rstd1[:], ("rstd1",))
            P.emit()

    def build(self):
        nc = self.nc
        from contextlib import ExitStack
        with nc.Block() as block:
            self.P = Plan(nc, block)
            P = self.P
            P.make_sems()
            with ExitStack() as stk:
                sb = lambda n, shp, dt: stk.enter_context(nc.sbuf_tensor(n, shp, dt))
                ps = lambda n, shp, dt: stk.enter_context(nc.psum_tensor(n, shp, dt))
                self.cst = {}
                for n, shp in [("conv_w4", [128, 320]), ("conv_b80", [128, 80]),
                               ("dt_bias_c", [128, 1]), ("a_log_c", [128, 1]),
                               ("d_pair", [128, 64]), ("norm_pair", [128, 64]),
                               ("b_base32", [128, 32]), ("identity", [128, 128]),
                               ("dmask", [128, 128]), ("msel", [128, 8])]:
                    self.cst[n] = sb("c_" + n, shp, F32)
                self.cst["e8"] = sb("c_e8", [8, 128], F32)
                self.idb = sb("c_idb", [128, 128], BF16)
                self.onesc = sb("c_ones", [128, 1], F32)
                self.onesr = sb("c_onesr", [1, 128], F32)
                self.flatr = sb("c_flatr", [1, 128], F32)
                self.onesb = sb("c_onesb", [1, 128], BF16)
                self.a_neg = sb("c_aneg", [128, 1], F32)
                self.epsc = sb("c_epsc", [128, 1], F32)
                self.zeroc = sb("c_zeroc", [128, 1], F32)
                self.m1 = sb("g_m1", [1, 2 * THF], F32)
                self.rstd1 = sb("g_rstd1", [1, 2 * THF], F32)
                self.pb = [ps(f"pb{i}", [128, 512], F32) for i in range(7)]
                self.pbT = ps("pbT", [128, 1024], BF16)
                mstk = ExitStack()
                odstk = ExitStack()
                try:
                    msb = lambda n, shp, dt: mstk.enter_context(nc.sbuf_tensor(n, shp, dt))
                    osb = lambda n, shp, dt: odstk.enter_context(nc.sbuf_tensor(n, shp, dt))
                    self.S = msb("g_S", [128, DI], F32)
                    self.od = osb("g_od", [128, T], BF16)
                    self.a_cs = osb("g_acs", [128, T], F32)
                    self.dac = osb("g_dac", [128, NCH * 128], F32)
                    self.odlbc = osb("g_odlbc", [128, NCH * 128], F32)
                    self.phase0()
                    P.emit()
                    if STOP_AFTER != "p0":
                        self.subpass1()
                    if STOP_AFTER is None or STOP_AFTER in ("ssd", "op", "ffn"):
                        self.exchange_and_sub2()
                        self.rrms = self.S[0:1, 0:2048]
                        self.m3R = self.S[0:1, 2048:4096]
                        self.r3R = self.S[0:1, 4096:6144]
                        self.sub2_ssd()
                        odstk.close()
                        if STOP_AFTER != "ssd":
                            self.sub2_rest()
                    for name, dst, src in self.tap_jobs:
                        P.dma("gpsimd", dst, src, reads=("sc" + name,), tag=f"tap {name}")
                    P.emit()
                finally:
                    for stk in (odstk, mstk):
                        try:
                            stk.close()
                        except Exception:
                            pass
                fc = P.final_counts()

                def drain(e, fc=fc):
                    for k, v in fc.items():
                        e.wait_ge(P.sems[k], v)
                block.sync(drain)
        return nc

    def exchange_and_sub2(self):
        nc, P, sc, cst = self.nc, self.P, self.sc, self.cst
        pb, pbT = self.pb, self.pbT
        from contextlib import ExitStack
        P.dma("sync", self.cc_in, self.S[:], reads=("S",), writes=("ccin",), tag="cc in")
        if SKIP_CC:
            for r in range(8):
                P.dma("gpsimd", self.cc_out[r * 128:(r + 1) * 128, :], self.cc_in,
                      reads=("ccin",), writes=("ccout",), tag=f"cc fake{r}")
        else:
            P.raw("gpsimd", "cc", 1, lambda e: e.collective_compute(
                "AllGather", ALU.bypass, replica_groups=[[0, 1, 2, 3, 4, 5, 6, 7]],
                ins=[self.cc_in.opt()], outs=[self.cc_out.opt()]),
                reads=("ccin",), writes=("ccout",), tag="allgather")
        P.emit()

    def ssd_phase(self, L):
        nc, P, sc, cst = self.nc, self.P, self.sc, self.cst
        pb = self.pb
        S0, sstg = L["S0"], L["sstg"]
        gB, gC2, gxdt2 = L["gB"], L["gC2"], L["gxdt2"]
        gxs2, gz2, SfG, Sfb = L["gxs2"], L["gz2"], L["SfG"], L["Sfb"]
        k2a, k2b, odf, l3, m3, Gm2 = L["k2a"], L["k2b"], L["odf"], L["l3"], L["m3"], L["Gm2"]
        cod, yst, ygb, ypb, ysq, acc = L["cod"], L["yst"], L["ygb"], L["ypb"], L["ysq"], L["acc"]
        rr2 = L["rr2"]
        rrms = self.rrms
        onesc, onesb = self.onesc, self.onesb
        dmask = cst["dmask"]

        def preload(ci, gi):
            if ci >= NCH:
                return
            sl = gi % 2
            t0c = ci * 256
            g = gi
            P.dma("sync", gB[:, sl * 256:(sl + 1) * 256],
                  sc["B"][g * 128:(g + 1) * 128, t0c:t0c + 256],
                  reads=("scB",), writes=(f"gB{sl}",), tag=f"Y{ci}.{g} Bld")
            for hv in range(2):
                P.dma("sync", gC2[:, sl * 512 + hv * 256:sl * 512 + (hv + 1) * 256],
                      sc["C"][g * 128:(g + 1) * 128, t0c:t0c + 256],
                      reads=("scC",), writes=(f"gC{sl}",), tag=f"Y{ci}.{g} Cld{hv}")
            for i in range(2):
                P.dma("sync", gxdt2[:, sl * 2048 + i * 1024:sl * 2048 + (i + 1) * 1024]
                      .rearrange("p (j t) -> p j t", j=8),
                      sc["xdt"][2 * ci + i, g * 8:(g + 1) * 8].rearrange("j p t -> p j t"),
                      reads=("scxdt",), writes=(f"gxdt{sl}",), tag=f"Y{ci}.{g} xdtld{i}")
            P.dma("sync", gxs2[:, sl * 2048:(sl + 1) * 2048].rearrange("p (pt t) -> p pt t", pt=8),
                  sc["xs"][g * 1024:(g + 1) * 1024, t0c:t0c + 256]
                  .rearrange("(pt p) t -> p pt t", p=128),
                  reads=("scxs",), writes=(f"gxs{sl}",), tag=f"Y{ci}.{g} xsld")
            P.dma("sync", gz2[:, sl * 2048:(sl + 1) * 2048].rearrange("p (pt t) -> p pt t", pt=8),
                  sc["z"][g * 1024:(g + 1) * 1024, t0c:t0c + 256]
                  .rearrange("(pt p) t -> p pt t", p=128),
                  reads=("scz",), writes=(f"gz{sl}",), tag=f"Y{ci}.{g} zld")
            P.dma("sync", SfG[:, sl * 1024:(sl + 1) * 1024],
                  sc["S"][ci][:, g * 1024:(g + 1) * 1024],
                  reads=("scS",), writes=(f"SfG{sl}",), tag=f"Y{ci}.{g} Sld")

        preload(0, 0)
        preload(0, 1)
        for c in range(NCH):
            t0c = c * 256
            for g in range(NG):
                sl = g % 2
                gsl = slice(g * 1024, (g + 1) * 1024)
                Ssl = slice(sl * 1024, (sl + 1) * 1024)
                dacAP = self.dac[:, c * 128 + g * 16:c * 128 + (g + 1) * 16] \
                    .broadcast_to([128, 16, 64])
                P.dma("sync", k2a[1:2, :], self.a_cs[g * 16:(g + 1) * 16, t0c:t0c + 256],
                      reads=("acs",), writes=("k2a1",), tag=f"Y{c}.{g} k2a")
                P.dma("sync", k2b[0:1, :], self.a_cs[g * 16:(g + 1) * 16, t0c:t0c + 256],
                      reads=("acs",), writes=("k2b0",), tag=f"Y{c}.{g} k2b")
                P.dma("sync", odf[0:1, :], self.od[g * 16:(g + 1) * 16, t0c:t0c + 256],
                      reads=("od",), writes=("odf", "r3R"), tag=f"Y{c}.{g} odf")
                P.op("vector", lambda e, gsl=gsl, dacAP=dacAP: e.tensor_tensor(
                    sstg[:], S0[:, gsl], dacAP, ALU.mult),
                    reads=("S0", f"dac{c}"), writes=("sstg",), tag=f"Y{c}.{g} Sdac")
                P.op("vector", lambda e, sl=sl, Ssl=Ssl: e.tensor_tensor(
                    Sfb[:, Ssl], SfG[:, Ssl], sstg[:], ALU.add),
                    reads=(f"SfG{sl}", "sstg"), writes=(f"Sfb{sl}",), tag=f"Y{c}.{g} Sfb")
                P.op("tensor", lambda e, sl=sl: e.matmul(
                    pb[5][:, 0:256], gB[:, sl * 256:sl * 256 + 128],
                    gC2[:, sl * 512:sl * 512 + 256],
                    start=True, stop=False, skip_group_check=True),
                    reads=(f"gB{sl}", f"gC{sl}"), writes=("pb5",), tag=f"Y{c}.{g} G1")
                P.op("tensor", lambda e, sl=sl: e.matmul(
                    pb[5][:, 256:384], gB[:, sl * 256 + 128:sl * 256 + 256],
                    gC2[:, sl * 512 + 128:sl * 512 + 256],
                    start=False, stop=True, skip_group_check=True),
                    reads=(f"gB{sl}", f"gC{sl}"), writes=("pb5",), tag=f"Y{c}.{g} G2")
                P.op("vector", lambda e: e.tensor_tensor(Gm2[:, 0:128], pb[5][:, 0:128],
                                                         dmask[:], ALU.mult),
                     reads=("pb5", "c_dmask"), writes=("Gm",), tag=f"Y{c}.{g} Gm0")
                P.op("vector", lambda e: e.tensor_copy(Gm2[:, 128:256], pb[5][:, 128:256]),
                     reads=("pb5",), writes=("Gm",), tag=f"Y{c}.{g} Gmm")
                P.op("vector", lambda e: e.tensor_tensor(Gm2[:, 256:384], pb[5][:, 256:384],
                                                         dmask[:], ALU.mult),
                     reads=("pb5", "c_dmask"), writes=("Gm",), tag=f"Y{c}.{g} Gm1")
                P.op("vector", lambda e: e.tensor_copy(Gm2[:, 384:768], Gm2[:, 0:384]),
                     reads=("Gm",), writes=("Gm",), tag=f"Y{c}.{g} Gmd")

                def stageA(pt):
                    ms = (pt % 2) * 768
                    cs = (pt % 2) * 512
                    kb = pt * 512
                    for par, db in ((0, 6), (1, 4)):
                        po = kb + par * 256
                        P.op("tensor", lambda e, po=po, db=db: e.matmul(
                            pb[db][:, 0:256], k2a[0:2, po:po + 128], k2b[0:2, po:po + 256],
                            start=True, stop=False, skip_group_check=True),
                            reads=("k2a1", "k2b0", "k2ac", "k2bc"), writes=(f"pb{db}",),
                            inc=False, tag=f"Y{c}.{g}.{pt} d{par}a")
                        P.op("tensor", lambda e, po=po, db=db: e.matmul(
                            pb[db][:, 256:384], k2a[0:2, po + 128:po + 256],
                            k2b[0:2, po + 128:po + 256],
                            start=False, stop=True, skip_group_check=True),
                            reads=("k2a1", "k2b0", "k2ac", "k2bc"), writes=(f"pb{db}",),
                            inc=True, tag=f"Y{c}.{g}.{pt} d{par}b")
                    P.op("tensor", lambda e, kb=kb: e.matmul(
                        pb[3][:, 0:512], onesb[0:1, 0:128], odf[0:1, kb:kb + 512],
                        start=True, stop=True),
                        reads=("odf", "onesb"), writes=("pb3",), tag=f"Y{c}.{g}.{pt} odb")
                    P.op("scalar", lambda e: e.activation(l3[:, 0:384], pb[6][:, 0:384], AF.Abs),
                         reads=("pb6",), writes=("l3",), tag=f"Y{c}.{g}.{pt} abs0")
                    P.op("scalar", lambda e: e.activation(l3[:, 384:768], pb[4][:, 0:384], AF.Abs),
                         reads=("pb4",), writes=("l3",), tag=f"Y{c}.{g}.{pt} abs1")
                    P.op("scalar", lambda e: e.activation(l3[:, 0:768], l3[:, 0:768],
                                                          AF.Exp, scale=-1.0),
                         reads=("l3",), writes=("l3",), tag=f"Y{c}.{g}.{pt} exp")
                    P.op("vector", lambda e, ms=ms: e.tensor_tensor(
                        m3[:, ms:ms + 768], Gm2[:, 0:768], l3[:, 0:768], ALU.mult),
                        reads=("Gm", "l3"), writes=(f"m3{pt%2}",), tag=f"Y{c}.{g}.{pt} m")
                    P.op("vector", lambda e, cs=cs, sl=sl: e.tensor_tensor(
                        cod[:, cs:cs + 512], gC2[:, sl * 512:(sl + 1) * 512], pb[3][:, 0:512],
                        ALU.mult),
                        reads=(f"gC{sl}", "pb3"), writes=(f"cod{pt%2}",), tag=f"Y{c}.{g}.{pt} cod")

                def stageB(pt):
                    ms = (pt % 2) * 768
                    cs = (pt % 2) * 512
                    yb = pt % 2
                    ps = (pt % 2) * 256
                    col = g * 8 + pt
                    xb = sl * 2048 + pt * 128
                    for par in range(2):
                        rsl = slice(par * 64, par * 64 + 64)
                        mo3 = ms + par * 384
                        hh = 2 * pt + par
                        P.op("tensor", lambda e, rsl=rsl, yb=yb, mo3=mo3, xb=xb, par=par: e.matmul(
                            pb[yb][rsl, 0:256], gxdt2[:, xb + par * 64:xb + par * 64 + 64],
                            m3[:, mo3:mo3 + 256],
                            start=True, stop=False, skip_group_check=True),
                            reads=(f"gxdt{sl}", f"m3{pt%2}"), writes=(f"pb{yb}",),
                            inc=False, tag=f"Y{c}.{g}.{pt}.{par} Y1")
                        P.op("tensor", lambda e, rsl=rsl, yb=yb, mo3=mo3, xb=xb, par=par: e.matmul(
                            pb[yb][rsl, 128:256],
                            gxdt2[:, 1024 + xb + par * 64:1024 + xb + par * 64 + 64],
                            m3[:, mo3 + 256:mo3 + 384],
                            start=False, stop=False, skip_group_check=True),
                            reads=(f"gxdt{sl}", f"m3{pt%2}"),
                            inc=False, tag=f"Y{c}.{g}.{pt}.{par} Y2")
                        P.op("tensor", lambda e, rsl=rsl, yb=yb, hh=hh, cs=cs, par=par, sl=sl: e.matmul(
                            pb[yb][rsl, 0:256], Sfb[:, sl * 1024 + hh * 64:sl * 1024 + (hh + 1) * 64],
                            cod[:, cs + par * 256:cs + (par + 1) * 256],
                            start=False, stop=True, skip_group_check=True),
                            reads=(f"Sfb{sl}", f"cod{pt%2}"), writes=(f"pb{yb}",),
                            inc=(par == 1), tag=f"Y{c}.{g}.{pt}.{par} Y3")
                    P.op("vector", lambda e, pt=pt, col=col, yb=yb, sl=sl: e.scalar_tensor_tensor(
                        yst[:], gxs2[:, sl * 2048 + pt * 256:sl * 2048 + (pt + 1) * 256],
                        cst["d_pair"][:, col:col + 1],
                        pb[yb][:, 0:256], ALU.mult, ALU.add),
                        reads=(f"gxs{sl}", "c_d_pair", f"pb{yb}"), writes=("yst",),
                        tag=f"Y{c}.{g}.{pt} ev")
                    P.op("vector", lambda e, pt=pt, sl=sl: e.tensor_tensor(
                        ygb[:], yst[:], gz2[:, sl * 2048 + pt * 256:sl * 2048 + (pt + 1) * 256],
                        ALU.mult),
                        reads=("yst", f"gz{sl}"), writes=("ygb",), tag=f"Y{c}.{g}.{pt} yg")
                    P.op("vector", lambda e, col=col, ps=ps: e.tensor_scalar(
                        ypb[:, ps:ps + 256], ygb[:],
                        cst["norm_pair"][:, col:col + 1], None, ALU.mult),
                        reads=("ygb", "c_norm_pair"), writes=(f"ypb{pt%2}",),
                        tag=f"Y{c}.{g}.{pt} yp")
                    P.dma("gpsimd", sc["yp"][col * 128:(col + 1) * 128, c * 256:(c + 1) * 256],
                          ypb[:, ps:ps + 256], reads=(f"ypb{pt%2}",),
                          writes=("scyp",), tag=f"Y{c}.{g}.{pt} st")
                    P.op("scalar", lambda e, ps=ps: e.activation(
                        ysq[:, ps:ps + 256], ygb[:], AF.Square),
                        reads=("ygb",), writes=(f"ysq{pt%2}",), tag=f"Y{c}.{g}.{pt} sq")
                    if g == 0 and pt == 0:
                        P.op("gpsimd", lambda e, ps=ps: e.tensor_copy(acc[:], ysq[:, ps:ps + 256]),
                             reads=(f"ysq{pt%2}",), writes=("acc",), tag=f"Y{c}.{g}.{pt} acc")
                    else:
                        P.op("gpsimd", lambda e, ps=ps: e.tensor_tensor(
                            acc[:], acc[:], ysq[:, ps:ps + 256], ALU.add),
                            reads=("acc", f"ysq{pt%2}"), writes=("acc",), tag=f"Y{c}.{g}.{pt} acc")

                for pt in range(8):
                    stageA(pt)
                    if pt >= 1:
                        stageB(pt - 1)
                stageB(7)
                preload(c + (g + 2) // 8, (g + 2) % 8)
            P.op("tensor", lambda e: e.matmul(pb[2][0:1, 0:256], onesc[:], acc[:],
                                              start=True, stop=True, skip_group_check=True),
                 reads=("acc", "ones"), writes=("pb2",), tag=f"Y{c} rmsmm")
            P.op("vector", lambda e: e.tensor_scalar(rr2[0:1, 0:256], pb[2][0:1, 0:256],
                                                     1.0 / DI, None, ALU.mult),
                 reads=("pb2",), writes=("rr2", "m3R"), tag=f"Y{c} rmean")
            P.op("scalar", lambda e: e.activation(rr2[0:1, 256:512], rr2[0:1, 0:256], AF.Ln,
                                                  bias=self.epsc[0:1, 0:1]),
                 reads=("rr2", "ceps"), writes=("rr2b", "m3R"), tag=f"Y{c} rln")
            P.op("scalar", lambda e, c=c: e.activation(rrms[0:1, c * 256:(c + 1) * 256],
                                                       rr2[0:1, 256:512], AF.Exp, scale=-0.5),
                 reads=("rr2b",), writes=("rrms",), tag=f"Y{c} rexp")

    def outproj_phase(self, L):
        nc, P, sc, cst = self.nc, self.P, self.sc, self.cst
        pb = self.pb
        ytq, wob = L["ytq"], L["wob"]
        u, v, w, xtt, sqt = L["u"], L["v"], L["w"], L["xtt"], L["sqt"]
        rbr, bm1, br1, bm2, br2 = L["rbr"], L["bm1"], L["br1"], L["bm2"], L["br2"]
        row = L["row"]
        onesc, onesr = self.onesc, self.onesr
        m3R, r3R = self.m3R, self.r3R
        for tq in range(4):
            th, q2 = tq // 2, tq % 2
            t0 = tq * 512
            for (name, src, dst) in [("rbr", self.rrms[0:1, t0:t0 + 512], rbr),
                                     ("bm1", self.m1[0:1, th * THF + HLO + q2 * 512:th * THF + HLO + q2 * 512 + 512], bm1),
                                     ("br1", self.rstd1[0:1, th * THF + HLO + q2 * 512:th * THF + HLO + q2 * 512 + 512], br1)]:
                P.op("tensor", lambda e, src=src: e.matmul(pb[4][:, 0:512], onesr[0:1, 0:128], src,
                                                           start=True, stop=True),
                     reads=("rrms", "m1", "rstd1", "onesr"), writes=("pb4",), tag=f"op{tq} b_{name}")
                P.op("vector", lambda e, dst=dst: e.tensor_copy(dst[:], pb[4][:, 0:512]),
                     reads=("pb4",), writes=((name, "bm2") if name == "bm1" else
                                             ((name, "br2") if name == "br1" else (name,))),
                     tag=f"op{tq} be_{name}")
            for kt in range(64):
                P.dma("sync", ytq[:, kt * 512:(kt + 1) * 512],
                      sc["yp"][kt * 128:(kt + 1) * 128, t0:t0 + 512],
                      reads=("scyp",), writes=(f"ytq{kt}",), tag=f"op{tq} yld{kt}")
            for oc in range(32):
                slot = oc % 2
                P.dma("gpsimd", wob[:, slot * 8192:(slot + 1) * 8192], self.ap["w_out_t"][oc],
                      reads=(), writes=(f"wob{slot}",), tag=f"op{tq}.{oc} wld")
                for kt in range(64):
                    P.op("tensor", lambda e, kt=kt, slot=slot, oc=oc: e.matmul(
                        pb[oc % 2][:, 0:512], wob[:, slot * 8192 + kt * 128:slot * 8192 + (kt + 1) * 128],
                        ytq[:, kt * 512:(kt + 1) * 512],
                        start=(kt == 0), stop=(kt == 63), skip_group_check=True),
                        reads=(f"wob{slot}", f"ytq{kt}"),
                        writes=((f"pb{oc%2}",) if kt in (0, 63) else ()),
                        inc=(kt == 63), tag=f"op{tq}.{oc} mm{kt}")
                P.dma("sync", xtt[:], self.ap["xT"][oc * 128:(oc + 1) * 128, HLO + t0:HLO + t0 + 512],
                      writes=("xtt",), tag=f"op{tq}.{oc} xld")
                P.op("vector", lambda e, oc=oc: e.tensor_tensor(u[:], pb[oc % 2][:, 0:512], rbr[:],
                                                                ALU.mult),
                     reads=(f"pb{oc%2}", "rbr"), writes=("u",), tag=f"op{tq}.{oc} t1")
                P.op("vector", lambda e: e.tensor_tensor(v[:], xtt[:], bm1[:], ALU.subtract),
                     reads=("xtt", "bm1"), writes=("v",), tag=f"op{tq}.{oc} t2")
                P.op("vector", lambda e: e.tensor_tensor(w[:], v[:], br1[:], ALU.mult),
                     reads=("v", "br1"), writes=("w",), tag=f"op{tq}.{oc} t3")
                P.op("vector", lambda e: e.tensor_tensor(v[:], u[:], w[:], ALU.add),
                     reads=("u", "w"), writes=("v",), tag=f"op{tq}.{oc} mo")
                P.dma("sync", sc["mo"][oc * 128:(oc + 1) * 128, t0:t0 + 512], v[:],
                      reads=("v",), writes=("scmo",), tag=f"op{tq}.{oc} most")
                P.op("scalar", lambda e: e.activation(sqt[:], v[:], AF.Square),
                     reads=("v",), writes=("sqt",), tag=f"op{tq}.{oc} sq")
                P.op("tensor", lambda e, oc=oc: e.matmul(pb[2][0:1, 0:512], onesc[:], v[:],
                                                         start=(oc == 0), stop=(oc == 31),
                                                         skip_group_check=True),
                     reads=("v", "ones"), writes=(("pb2",) if oc in (0, 31) else ()),
                     inc=True, tag=f"op{tq}.{oc} smm")
                P.op("tensor", lambda e, oc=oc: e.matmul(pb[3][0:1, 0:512], onesc[:], sqt[:],
                                                         start=(oc == 0), stop=(oc == 31),
                                                         skip_group_check=True),
                     reads=("sqt", "ones"), writes=(("pb3",) if oc in (0, 31) else ()),
                     inc=True, tag=f"op{tq}.{oc} qmm")
            P.op("vector", lambda e: e.tensor_scalar(row[0:1, 0:512], pb[2][0:1, 0:512],
                                                     1.0 / DM, None, ALU.mult),
                 reads=("pb2",), writes=("rowm",), tag=f"op{tq} mean")
            P.op("vector", lambda e: e.tensor_scalar(row[0:1, 512:1024], pb[3][0:1, 0:512],
                                                     1.0 / DM, None, ALU.mult),
                 reads=("pb3",), writes=("rowq",), tag=f"op{tq} msq")
            P.op("vector", lambda e: e.tensor_tensor(row[0:1, 1024:1536], row[0:1, 0:512],
                                                     row[0:1, 0:512], ALU.mult),
                 reads=("rowm",), writes=("rowmm",), tag=f"op{tq} m2")
            P.op("vector", lambda e: e.tensor_tensor(row[0:1, 512:1024], row[0:1, 512:1024],
                                                     row[0:1, 1024:1536], ALU.subtract),
                 reads=("rowq", "rowmm"), writes=("rowq",), tag=f"op{tq} var")
            P.op("scalar", lambda e: e.activation(row[0:1, 1024:1536], row[0:1, 512:1024],
                                                  AF.Ln, bias=self.epsc[0:1, 0:1]),
                 reads=("rowq", "ceps"), writes=("rowmm",), tag=f"op{tq} ln")
            P.op("scalar", lambda e: e.activation(row[0:1, 512:1024], row[0:1, 1024:1536],
                                                  AF.Exp, scale=-0.5),
                 reads=("rowmm",), writes=("rowq",), tag=f"op{tq} rstd")
            P.op("tensor", lambda e: e.matmul(pb[4][:, 0:512], onesr[0:1, 0:128], row[0:1, 0:512],
                                              start=True, stop=True),
                 reads=("rowm", "onesr"), writes=("pb4",), tag=f"op{tq} bm2")
            P.op("vector", lambda e: e.tensor_copy(bm2[:], pb[4][:, 0:512]),
                 reads=("pb4",), writes=("bm2", "bm1"), tag=f"op{tq} bm2e")
            P.op("tensor", lambda e: e.matmul(pb[4][:, 0:512], onesr[0:1, 0:128], row[0:1, 512:1024],
                                              start=True, stop=True),
                 reads=("rowq", "onesr"), writes=("pb4",), tag=f"op{tq} br2")
            P.op("vector", lambda e: e.tensor_copy(br2[:], pb[4][:, 0:512]),
                 reads=("pb4",), writes=("br2", "br1"), tag=f"op{tq} br2e")
            for oc in range(32):
                P.dma("sync", u[:], sc["mo"][oc * 128:(oc + 1) * 128, t0:t0 + 512],
                      reads=("scmo",), writes=("u",), tag=f"pb{tq}.{oc} mold")
                P.dma("sync", xtt[:], self.ap["xT"][oc * 128:(oc + 1) * 128, HLO + t0:HLO + t0 + 512],
                      writes=("xtt",), tag=f"pb{tq}.{oc} xld")
                P.op("vector", lambda e: e.tensor_tensor(v[:], u[:], bm2[:], ALU.subtract),
                     reads=("u", "bm2"), writes=("v",), tag=f"pb{tq}.{oc} s1")
                P.op("vector", lambda e: e.tensor_tensor(w[:], v[:], br2[:], ALU.mult),
                     reads=("v", "br2"), writes=("w",), tag=f"pb{tq}.{oc} s2")
                P.op("vector", lambda e: e.tensor_tensor(v[:], w[:], xtt[:], ALU.add),
                     reads=("w", "xtt"), writes=("v",), tag=f"pb{tq}.{oc} x2")
                P.dma("sync", sc["x2"][oc * 128:(oc + 1) * 128, t0:t0 + 512], v[:],
                      reads=("v",), writes=("scx2",), tag=f"pb{tq}.{oc} x2st")
                P.op("scalar", lambda e: e.activation(sqt[:], v[:], AF.Square),
                     reads=("v",), writes=("sqt",), tag=f"pb{tq}.{oc} sq")
                P.op("tensor", lambda e, oc=oc: e.matmul(pb[2][0:1, 0:512], onesc[:], v[:],
                                                         start=(oc == 0), stop=(oc == 31),
                                                         skip_group_check=True),
                     reads=("v", "ones"), writes=(("pb2",) if oc in (0, 31) else ()),
                     inc=True, tag=f"pb{tq}.{oc} smm")
                P.op("tensor", lambda e, oc=oc: e.matmul(pb[3][0:1, 0:512], onesc[:], sqt[:],
                                                         start=(oc == 0), stop=(oc == 31),
                                                         skip_group_check=True),
                     reads=("sqt", "ones"), writes=(("pb3",) if oc in (0, 31) else ()),
                     inc=True, tag=f"pb{tq}.{oc} qmm")
            P.op("vector", lambda e, t0=t0: e.tensor_scalar(m3R[0:1, t0:t0 + 512], pb[2][0:1, 0:512],
                                                            1.0 / DM, None, ALU.mult),
                 reads=("pb2",), writes=("m3R",), tag=f"pb{tq} mean")
            P.op("vector", lambda e: e.tensor_scalar(row[0:1, 512:1024], pb[3][0:1, 0:512],
                                                     1.0 / DM, None, ALU.mult),
                 reads=("pb3",), writes=("rowq",), tag=f"pb{tq} msq")
            P.op("vector", lambda e, t0=t0: e.tensor_tensor(row[0:1, 1024:1536], m3R[0:1, t0:t0 + 512],
                                                            m3R[0:1, t0:t0 + 512], ALU.mult),
                 reads=("m3R",), writes=("rowmm",), tag=f"pb{tq} m2")
            P.op("vector", lambda e: e.tensor_tensor(row[0:1, 512:1024], row[0:1, 512:1024],
                                                     row[0:1, 1024:1536], ALU.subtract),
                 reads=("rowq", "rowmm"), writes=("rowq",), tag=f"pb{tq} var")
            P.op("scalar", lambda e: e.activation(row[0:1, 1024:1536], row[0:1, 512:1024],
                                                  AF.Ln, bias=self.epsc[0:1, 0:1]),
                 reads=("rowq", "ceps"), writes=("rowmm",), tag=f"pb{tq} ln")
            P.op("scalar", lambda e, t0=t0: e.activation(r3R[0:1, t0:t0 + 512], row[0:1, 1024:1536],
                                                         AF.Exp, scale=-0.5),
                 reads=("rowmm",), writes=("r3R",), tag=f"pb{tq} rstd")

    def ffn_phase(self, L):
        nc, P, sc, cst = self.nc, self.P, self.sc, self.cst
        pb = self.pb
        x2l, wbb = L["x2l"], L["wbb"]
        u, v, w, sqt = L["u2"], L["v2"], L["w2"], L["sqt2"]
        bma, bmb = L["bma"], L["bmb"]
        lg, ltm, comb, cfm, ce, hgb = L["lg"], L["ltm"], L["comb"], L["cfm"], L["ce"], L["hgb"]
        row = None
        onesc, onesr = self.onesc, self.onesr
        m3R, r3R = self.m3R, self.r3R
        idf = cst["identity"]
        P.dma("gpsimd", self.wrT[:], self.ap["w_router_t"], writes=("c_wrT",), tag="ld_wrT")
        P.dma("gpsimd", self.aallT[:], self.ap["a_all_t"], writes=("c_aallT",), tag="ld_aallT")
        P.dma("gpsimd", self.bflat[:], self.ap["b_flat"], writes=("c_bflat",), tag="ld_bflat")
        for th in range(2):
            t0 = th * 1024
            for q2 in range(2):
                tq0 = t0 + q2 * 512
                P.op("tensor", lambda e, tq0=tq0: e.matmul(pb[4][:, 0:512], onesr[0:1, 0:128],
                                                           m3R[0:1, tq0:tq0 + 512], start=True, stop=True),
                     reads=("m3R", "onesr"), writes=("pb4",), tag=f"f{th}.{q2} bm3")
                P.op("vector", lambda e: e.tensor_copy(bma[:], pb[4][:, 0:512]),
                     reads=("pb4",), writes=("bma",), tag=f"f{th}.{q2} bm3e")
                P.op("tensor", lambda e, tq0=tq0: e.matmul(pb[4][:, 0:512], onesr[0:1, 0:128],
                                                           r3R[0:1, tq0:tq0 + 512], start=True, stop=True),
                     reads=("r3R", "onesr"), writes=("pb4",), tag=f"f{th}.{q2} br3")
                P.op("vector", lambda e: e.tensor_copy(bmb[:], pb[4][:, 0:512]),
                     reads=("pb4",), writes=("bmb",), tag=f"f{th}.{q2} br3e")
                for oc in range(32):
                    P.dma("sync", u[:], sc["x2"][oc * 128:(oc + 1) * 128, tq0:tq0 + 512],
                          reads=("scx2",), writes=("u2",), tag=f"f{th}.{q2}.{oc} x2ld")
                    P.op("vector", lambda e: e.tensor_tensor(v[:], u[:], bma[:], ALU.subtract),
                         reads=("u2", "bma"), writes=("v2",), tag=f"f{th}.{q2}.{oc} s")
                    P.op("vector", lambda e, oc=oc, q2=q2: e.tensor_tensor(
                        x2l[:, oc * 1024 + q2 * 512:oc * 1024 + (q2 + 1) * 512], v[:], bmb[:], ALU.mult),
                        reads=("v2", "bmb"), writes=(f"x2l{oc}",), tag=f"f{th}.{q2}.{oc} m")
            for nt in range(2):
                for kt in range(NKT):
                    P.op("tensor", lambda e, nt=nt, kt=kt: e.matmul(
                        pb[5 + nt][0:8, 0:512], self.wrT[:, kt * 8:(kt + 1) * 8],
                        x2l[:, kt * 1024 + nt * 512:kt * 1024 + (nt + 1) * 512],
                        start=(kt == 0), stop=(kt == NKT - 1), skip_group_check=True),
                        reads=("c_wrT", f"x2l{kt}"),
                        writes=((f"pb{5+nt}",) if kt in (0, NKT - 1) else ()),
                        inc=True, tag=f"f{th} lg{nt}.{kt}")
                P.op("vector", lambda e, nt=nt: e.tensor_copy(lg[0:8, nt * 512:(nt + 1) * 512],
                                                              pb[5 + nt][0:8, 0:512]),
                     reads=(f"pb{5+nt}",), writes=("lg",), tag=f"f{th} lge{nt}")
            for tt in range(8):
                P.op("tensor", lambda e, tt=tt: e.transpose(pb[4][0:128, 0:8],
                                                            lg[0:8, tt * 128:(tt + 1) * 128],
                                                            idf[0:8, 0:8]),
                     reads=("lg", "c_identity"), writes=("pb4",), tag=f"f{th}.{tt} ltr")
                P.op("vector", lambda e: e.tensor_copy(ltm[:, 0:8], pb[4][0:128, 0:8]),
                     reads=("pb4",), writes=("ltm",), tag=f"f{th}.{tt} ltre")
                P.op("vector", lambda e: e.reduce_max(ltm[:, 8:9], ltm[:, 0:8], axis=mybir.AxisListType.X),
                     reads=("ltm",), writes=("ltm1",), tag=f"f{th}.{tt} max1")
                P.op("vector", lambda e: e.tensor_scalar(comb[:, 0:8], ltm[:, 0:8], ltm[:, 8:9],
                                                         None, ALU.is_ge),
                     reads=("ltm", "ltm1"), writes=("mask1", "combf"), tag=f"f{th}.{tt} msk1")
                P.op("vector", lambda e: e.scalar_tensor_tensor(comb[:, 8:16], comb[:, 0:8], -1e30,
                                                                ltm[:, 0:8], ALU.mult, ALU.add),
                     reads=("mask1", "ltm"), writes=("l2b",), tag=f"f{th}.{tt} l2")
                P.op("vector", lambda e: e.reduce_max(ltm[:, 9:10], comb[:, 8:16],
                                                      axis=mybir.AxisListType.X),
                     reads=("l2b",), writes=("ltm2",), tag=f"f{th}.{tt} max2")
                P.op("vector", lambda e: e.tensor_scalar(comb[:, 16:24], comb[:, 8:16], ltm[:, 9:10],
                                                         None, ALU.is_ge),
                     reads=("l2b", "ltm2"), writes=("mask2",), tag=f"f{th}.{tt} msk2")
                P.op("vector", lambda e: e.tensor_tensor(ltm[:, 10:11], ltm[:, 8:9], ltm[:, 9:10],
                                                         ALU.subtract),
                     reads=("ltm1", "ltm2"), writes=("dgap",), tag=f"f{th}.{tt} d")
                P.op("scalar", lambda e: e.activation(ltm[:, 11:12], ltm[:, 10:11], AF.Sigmoid),
                     reads=("dgap",), writes=("p1",), tag=f"f{th}.{tt} sig")
                P.op("vector", lambda e: e.tensor_scalar(ltm[:, 12:13], ltm[:, 11:12], 2.0, None,
                                                         ALU.mult),
                     reads=("p1",), writes=("p1s",), tag=f"f{th}.{tt} p1s")
                P.op("vector", lambda e: e.tensor_scalar(ltm[:, 13:14], ltm[:, 11:12], -2.0, 2.0,
                                                         ALU.mult, ALU.add),
                     reads=("p1",), writes=("p2s",), tag=f"f{th}.{tt} p2s")
                P.op("vector", lambda e: e.tensor_scalar(comb[:, 24:32], comb[:, 0:8], ltm[:, 12:13],
                                                         None, ALU.mult),
                     reads=("mask1", "p1s"), writes=("c1",), tag=f"f{th}.{tt} c1")
                P.op("vector", lambda e: e.tensor_scalar(comb[:, 8:16], comb[:, 16:24], ltm[:, 13:14],
                                                         None, ALU.mult),
                     reads=("mask2", "p2s"), writes=("c2",), tag=f"f{th}.{tt} c2")
                P.op("vector", lambda e: e.tensor_tensor(comb[:, 0:8], comb[:, 24:32], comb[:, 8:16],
                                                         ALU.add),
                     reads=("c1", "c2"), writes=("combf",), tag=f"f{th}.{tt} cmb")
                P.op("tensor", lambda e: e.transpose(pb[4][0:8, 0:128], comb[:, 0:8], idf[:]),
                     reads=("combf", "c_identity"), writes=("pb4",), tag=f"f{th}.{tt} ctr")
                P.op("vector", lambda e, tt=tt: e.tensor_copy(cfm[0:8, tt * 128:(tt + 1) * 128],
                                                              pb[4][0:8, 0:128]),
                     reads=("pb4",), writes=("cfm",), tag=f"f{th}.{tt} ctre")
            for nt in range(2):
                P.op("tensor", lambda e, nt=nt: e.matmul(pb[4][:, 0:512], cst["e8"][0:8, :],
                                                         cfm[0:8, nt * 512:(nt + 1) * 512],
                                                         start=True, stop=True),
                     reads=("cfm", "c_e8"), writes=("pb4",), tag=f"f{th} ex{nt}")
                P.op("vector", lambda e, nt=nt: e.tensor_copy(ce[:, nt * 512:(nt + 1) * 512],
                                                              pb[4][:, 0:512]),
                     reads=("pb4",), writes=("ce",), tag=f"f{th} exe{nt}")
            for nt in range(2):
                for kt in range(NKT):
                    P.op("tensor", lambda e, nt=nt, kt=kt: e.matmul(
                        pb[5][:, 0:512], self.aallT[:, kt * 128:(kt + 1) * 128],
                        x2l[:, kt * 1024 + nt * 512:kt * 1024 + (nt + 1) * 512],
                        start=(kt == 0), stop=(kt == NKT - 1), skip_group_check=True),
                        reads=("c_aallT", f"x2l{kt}"),
                        writes=(("pb5",) if kt in (0, NKT - 1) else ()),
                        inc=True, tag=f"f{th} h{nt}.{kt}")
                P.op("vector", lambda e, nt=nt: e.tensor_tensor(hgb[:, nt * 512:(nt + 1) * 512],
                                                                pb[5][:, 0:512],
                                                                ce[:, nt * 512:(nt + 1) * 512], ALU.mult),
                     reads=("pb5", "ce"), writes=("hgb",), tag=f"f{th} hg{nt}")
            for oc in range(32):
                slot = oc % 2
                P.dma("gpsimd", wbb[:, slot * 4096:(slot + 1) * 4096], self.ap["w_base_t"][oc],
                      reads=(), writes=(f"wbb{slot}",), tag=f"f{th}.{oc} wld")
                for nt in range(2):
                    for kt in range(NKT):
                        P.op("tensor", lambda e, nt=nt, kt=kt, slot=slot, oc=oc: e.matmul(
                            pb[oc % 2][:, 0:512],
                            wbb[:, slot * 4096 + kt * 128:slot * 4096 + (kt + 1) * 128],
                            x2l[:, kt * 1024 + nt * 512:kt * 1024 + (nt + 1) * 512],
                            start=(kt == 0), stop=False, skip_group_check=True),
                            reads=(f"wbb{slot}", f"x2l{kt}"),
                            writes=((f"pb{oc%2}",) if kt == 0 else ()),
                            inc=False, tag=f"f{th}.{oc} bmm{nt}.{kt}")
                    P.op("tensor", lambda e, nt=nt, oc=oc: e.matmul(
                        pb[oc % 2][:, 0:512], self.bflat[:, oc * 128:(oc + 1) * 128],
                        hgb[:, nt * 512:(nt + 1) * 512],
                        start=False, stop=True, skip_group_check=True),
                        reads=("c_bflat", "hgb"), writes=(f"pb{oc%2}",),
                        inc=True, tag=f"f{th}.{oc} lora{nt}")
                    P.op("vector", lambda e, nt=nt, oc=oc: e.tensor_scalar(
                        u[:], pb[oc % 2][:, 0:512], cst["b_base32"][:, oc:oc + 1], None, ALU.add),
                        reads=(f"pb{oc%2}", "c_b_base32"), writes=("u2",), tag=f"f{th}.{oc} bias{nt}")
                    P.dma("sync", sc["ffn"][oc * 128:(oc + 1) * 128, t0 + nt * 512:t0 + (nt + 1) * 512],
                          u[:], reads=("u2",), writes=("scffn",), tag=f"f{th}.{oc} fst{nt}")
                    P.op("scalar", lambda e: e.activation(sqt[:], u[:], AF.Square),
                         reads=("u2",), writes=("sqt2",), tag=f"f{th}.{oc} sq{nt}")
                    sbk = 2 if nt == 0 else 6
                    qbk = 3 if nt == 0 else 4
                    P.op("tensor", lambda e, oc=oc, sbk=sbk: e.matmul(
                        pb[sbk][0:1, 0:512], onesc[:], u[:],
                        start=(oc == 0), stop=(oc == 31), skip_group_check=True),
                        reads=("u2", "ones"), writes=((f"pb{sbk}",) if oc in (0, 31) else ()),
                        inc=True, tag=f"f{th}.{oc} smm{nt}")
                    P.op("tensor", lambda e, oc=oc, qbk=qbk: e.matmul(
                        pb[qbk][0:1, 0:512], onesc[:], sqt[:],
                        start=(oc == 0), stop=(oc == 31), skip_group_check=True),
                        reads=("sqt2", "ones"), writes=((f"pb{qbk}",) if oc in (0, 31) else ()),
                        inc=True, tag=f"f{th}.{oc} qmm{nt}")
            rowm = self.m3R[0:1, 0:1024]
            rowq = self.r3R[0:1, 0:1024]
            rowmm = self.m3R[0:1, 1024:2048]
            P.op("vector", lambda e, rowm=rowm: e.tensor_scalar(rowm[0:1, 0:512], pb[2][0:1, 0:512],
                                                                1.0 / DM, None, ALU.mult),
                 reads=("pb2",), writes=("m3R",), tag=f"f{th} mean0")
            P.op("vector", lambda e, rowm=rowm: e.tensor_scalar(rowm[0:1, 512:1024], pb[6][0:1, 0:512],
                                                                1.0 / DM, None, ALU.mult),
                 reads=("pb6",), writes=("m3R",), tag=f"f{th} mean1")
            P.op("vector", lambda e, rowq=rowq: e.tensor_scalar(rowq[0:1, 0:512], pb[3][0:1, 0:512],
                                                                1.0 / DM, None, ALU.mult),
                 reads=("pb3",), writes=("r3R",), tag=f"f{th} msq0")
            P.op("vector", lambda e, rowq=rowq: e.tensor_scalar(rowq[0:1, 512:1024], pb[4][0:1, 0:512],
                                                                1.0 / DM, None, ALU.mult),
                 reads=("pb4",), writes=("r3R",), tag=f"f{th} msq1")
            P.op("vector", lambda e, rowm=rowm, rowmm=rowmm: e.tensor_tensor(
                rowmm, rowm, rowm, ALU.mult),
                 reads=("m3R",), writes=("m3R",), tag=f"f{th} m2")
            P.op("vector", lambda e, rowq=rowq, rowmm=rowmm: e.tensor_tensor(
                rowq, rowq, rowmm, ALU.subtract),
                 reads=("r3R", "m3R"), writes=("r3R",), tag=f"f{th} var")
            P.op("scalar", lambda e, rowq=rowq, rowmm=rowmm: e.activation(
                rowmm, rowq, AF.Ln, bias=self.epsc[0:1, 0:1]),
                 reads=("r3R", "ceps"), writes=("m3R",), tag=f"f{th} ln")
            P.op("scalar", lambda e, rowq=rowq, rowmm=rowmm: e.activation(
                rowq, rowmm, AF.Exp, scale=-0.5),
                 reads=("m3R",), writes=("r3R",), tag=f"f{th} rstd")
            for q2 in range(2):
                tq0 = t0 + q2 * 512
                P.op("tensor", lambda e, q2=q2: e.matmul(pb[4][:, 0:512], onesr[0:1, 0:128],
                                                         self.m3R[0:1, q2 * 512:(q2 + 1) * 512],
                                                         start=True, stop=True),
                     reads=("m3R", "onesr"), writes=("pb4",), tag=f"f{th}.{q2} bm4")
                P.op("vector", lambda e: e.tensor_copy(bma[:], pb[4][:, 0:512]),
                     reads=("pb4",), writes=("bma",), tag=f"f{th}.{q2} bm4e")
                P.op("tensor", lambda e, q2=q2: e.matmul(pb[4][:, 0:512], onesr[0:1, 0:128],
                                                         self.r3R[0:1, q2 * 512:(q2 + 1) * 512],
                                                         start=True, stop=True),
                     reads=("r3R", "onesr"), writes=("pb4",), tag=f"f{th}.{q2} br4")
                P.op("vector", lambda e: e.tensor_copy(bmb[:], pb[4][:, 0:512]),
                     reads=("pb4",), writes=("bmb",), tag=f"f{th}.{q2} br4e")
                for oc in range(32):
                    P.dma("sync", u[:], sc["ffn"][oc * 128:(oc + 1) * 128, tq0:tq0 + 512],
                          reads=("scffn",), writes=("u2",), tag=f"o{th}.{q2}.{oc} fld")
                    P.dma("sync", w[:], sc["x2"][oc * 128:(oc + 1) * 128, tq0:tq0 + 512],
                          reads=("scx2",), writes=("w2",), tag=f"o{th}.{q2}.{oc} x2ld")
                    P.op("vector", lambda e: e.tensor_tensor(v[:], u[:], bma[:], ALU.subtract),
                         reads=("u2", "bma"), writes=("v2",), tag=f"o{th}.{q2}.{oc} s1")
                    P.op("vector", lambda e: e.tensor_tensor(u[:], v[:], bmb[:], ALU.mult),
                         reads=("v2", "bmb"), writes=("u2",), tag=f"o{th}.{q2}.{oc} s2")
                    P.op("vector", lambda e: e.tensor_tensor(v[:], u[:], w[:], ALU.add),
                         reads=("u2", "w2"), writes=("v2",), tag=f"o{th}.{q2}.{oc} out")
                    P.dma("sync", self.out_t[oc * 128:(oc + 1) * 128, tq0:tq0 + 512], v[:],
                          reads=("v2",), writes=("outT",), tag=f"o{th}.{q2}.{oc} ost")

    def sub2_ssd(self):
        nc, P = self.nc, self.P
        P.barrier("b_ssd")
        from contextlib import ExitStack
        if True:
            with ExitStack() as st:
                sbs = lambda n, shp, dt: st.enter_context(nc.sbuf_tensor(n, shp, dt))
                S0 = sbs("S0", [128, DI], F32)
                with nc.sbuf_tensor("stg", [128, DI], F32) as stg:
                    P.op("vector", lambda e: e.memset(S0[:], 0.0), writes=("S0",), tag="S0 z")
                    for r in range(8):
                        P.dma("sync", stg[:], self.cc_out[r * 128:(r + 1) * 128, :],
                              reads=("ccout",), writes=("stg",), tag=f"S0 ld{r}")
                        P.op("vector", lambda e, r=r: e.scalar_tensor_tensor(
                            S0[:], stg[:], self.cst["msel"][:, r:r + 1], S0[:], ALU.mult, ALU.add),
                            reads=("stg", "c_msel", "S0"), writes=("S0",), tag=f"S0 acc{r}")
                    P.barrier("b_stg")
                    P.emit()
                L = dict(S0=S0,
                         sstg=sbs("sstg", [128, 1024], F32),
                         SfG=sbs("SfG", [128, 2048], F32),
                         Sfb=sbs("Sfb", [128, 2048], BF16),
                         gB=sbs("gB", [128, 512], BF16), gC2=sbs("gC2", [128, 1024], BF16),
                         gxdt2=sbs("gxdt2", [128, 4096], BF16),
                         gxs2=sbs("gxs2", [128, 4096], BF16),
                         gz2=sbs("gz2", [128, 4096], BF16),
                         k2a=sbs("k2a", [2, 4096], F32), k2b=sbs("k2b", [2, 4096], F32),
                         odf=self.r3R[0:1, 0:2048].bitcast(BF16),
                         l3=sbs("l3", [128, 768], F32), m3=sbs("m3", [128, 1536], BF16),
                         Gm2=sbs("Gm2", [128, 768], F32),
                         cod=sbs("cod", [128, 1024], BF16),
                         yst=sbs("yst", [128, 256], F32),
                         ygb=sbs("ygb", [128, 256], F32), ypb=sbs("ypb", [128, 512], BF16),
                         ysq=sbs("ysq", [128, 512], F32), acc=sbs("acc", [128, 256], F32),
                         rr2=self.m3R[0:1, 0:512])
                P.op("vector", lambda e: e.memset(L["k2a"][0:1, :], 1.0), writes=("k2ac",), tag="k2a1")
                P.op("vector", lambda e: e.memset(L["k2b"][0:1, :], -1.0), writes=("k2b0",), tag="k2bn")
                P.dma("sync", L["k2b"][1:2, :], L["k2b"][0:1, :], reads=("k2b0",),
                      writes=("k2bc",), tag="k2bc cp")
                if STOP_AFTER != "ssd":
                    pass
                self.ssd_phase(L)
                P.emit()

    def sub2_rest(self):
        nc, P = self.nc, self.P
        P.barrier("b_op")
        from contextlib import ExitStack
        if True:
            with ExitStack() as st:
                sbs = lambda n, shp, dt: st.enter_context(nc.sbuf_tensor(n, shp, dt))
                L = dict(ytq=sbs("ytq", [128, 64 * 512], BF16),
                         wob=sbs("wob", [128, 2 * 8192], BF16),
                         u=sbs("u", [128, 512], F32), v=sbs("v", [128, 512], F32),
                         w=sbs("w", [128, 512], F32), xtt=sbs("xtt", [128, 512], F32),
                         sqt=sbs("sqt", [128, 512], F32),
                         rbr=sbs("rbr", [128, 512], F32), bm1=sbs("bm1", [128, 512], F32),
                         br1=sbs("br1", [128, 512], F32), row=sbs("rowb", [1, 1536], F32))
                L["bm2"] = L["bm1"]
                L["br2"] = L["br1"]
                self.outproj_phase(L)
                P.emit()
            if STOP_AFTER == "op":
                return
            P.barrier("b_ffn")
            with ExitStack() as st:
                sbs = lambda n, shp, dt: st.enter_context(nc.sbuf_tensor(n, shp, dt))
                self.wrT = sbs("c_wrT", [128, 256], BF16)
                self.aallT = sbs("c_aallT", [128, 4096], BF16)
                self.bflat = sbs("c_bflat", [128, 4096], BF16)
                L = dict(x2l=sbs("x2l", [128, 32 * 1024], BF16),
                         wbb=sbs("wbb", [128, 2 * 4096], BF16),
                         u2=sbs("u2", [128, 512], F32), v2=sbs("v2", [128, 512], F32),
                         w2=sbs("w2", [128, 512], F32), sqt2=sbs("sqt2", [128, 512], F32),
                         bma=sbs("bma", [128, 512], F32), bmb=sbs("bmb", [128, 512], F32),
                         lg=sbs("lg", [8, 1024], F32), ltm=sbs("ltm", [128, 16], F32),
                         comb=sbs("comb", [128, 32], F32), cfm=sbs("cfm", [8, 1024], F32),
                         ce=sbs("ce", [128, 1024], F32), hgb=sbs("hgb", [128, 1024], BF16))
                self.ffn_phase(L)
                P.emit()


def kernel(**inputs):
    sh, per = _stage_host(inputs)
    nc = bass.Bass("TRN2", target_bir_lowering=False)
    mk = MK(nc, list(DEBUG_TAPS))
    for name in list(DEBUG_TAPS):
        mk.tap_sc(name)
    mk.build()
    in_maps = []
    for c in range(N_CORES):
        m = dict(sh)
        m.update(per[c])
        in_maps.append(m)
    res = run_bass_kernel_spmd(nc, in_maps, core_ids=list(range(N_CORES)))
    global LAST_TAPS
    LAST_TAPS = [{k: v for k, v in r.items() if k.startswith("tap_")} for r in res.results]
    outs = [r["out_t"] for r in res.results]
    full = np.zeros((4, 4096, 4096), np.float32)
    for c in range(N_CORES):
        s, h = c // 2, c % 2
        full[s, h * T:(h + 1) * T, :] = outs[c].T
    return full


# version 12
# speedup vs baseline: 1.0385x; 1.0385x over previous
import numpy as np
import concourse.bass as bass
import concourse.mybir as mybir
from concourse.bass_utils import run_bass_kernel_spmd
from concourse.alu_op_type import AluOpType as ALU

F32 = mybir.dt.float32
BF16 = mybir.dt.bfloat16
AF = mybir.ActivationFunctionType

DM, DI = 4096, 8192
NH, HP, NG, DS = 128, 64, 8, 128
CK, CH = 4, 256
EPS = 1e-5
T, TH, HLO = 2048, 1024, 8
THF = TH + HLO
NT3 = THF // 3
NKT = DM // 128
NCC = 18560 // 128
NCH = T // CH

DEBUG_TAPS = []
STOP_AFTER = None
N_CORES = 8
SKIP_CC = False
LAST_TAPS = {}

_ENG = ("sync", "gpsimd", "tensor", "vector", "scalar")
_QUEUE = {"sync": "qs", "gpsimd": "qg", "scalar": "qa"}


class Plan:

    def __init__(self, nc, block):
        self.nc = nc
        self.block = block
        self.instrs = []
        self.writer = {}
        self.readers = {}
        self.sems = {}
        self.emitted = 0
        self.hwm = {}
        self._count = {}
        self._ctx = []

    def make_sems(self):
        keys = ["tensor", "vector", "scalar", "gpsimd", "sync", "cc"]
        for q in ("qs", "qg", "qa"):
            keys += [f"{q}{i}" for i in range(4)]
        for k in keys:
            cm = self.nc.semaphore("sem_" + k)
            self.sems[k] = cm.__enter__()
            self._ctx.append(cm)
        self.qctr = {"qs": 0, "qg": 0, "qa": 0}

    def _add(self, eng, semkey, inc_n, fn, reads, writes, tag):
        idx = len(self.instrs)
        dep = set()
        for r in reads:
            w = self.writer.get(r)
            if w is not None:
                dep.add(w)
        for r in writes:
            w = self.writer.get(r)
            if w is not None:
                dep.add(w)
            for rd in self.readers.get(r, ()):
                dep.add(rd)
        for r in reads:
            self.readers.setdefault(r, []).append(idx)
        for r in writes:
            self.writer[r] = idx
            self.readers[r] = []
        self.instrs.append(dict(idx=idx, eng=eng, semkey=semkey, inc=inc_n, fn=fn,
                                tag=tag, deps=dep, waits=[], cum=None, xwait=None))
        return idx

    def op(self, eng, fn, reads=(), writes=(), inc=True, tag=""):
        return self._add(eng, eng, 1 if inc else 0, fn, reads, writes, tag)

    def dma(self, eng, out, in_, reads=(), writes=(), tag=""):
        fn = lambda e, o=out, i=in_: e.dma_start(out=o, in_=i)
        q = _QUEUE[eng]
        i = self.qctr[q]
        self.qctr[q] += 1
        idx = self._add(eng, f"{q}{i % 4}", 16, fn, reads, writes, tag)
        if i >= 4:
            self.instrs[idx]["xwait"] = (f"{q}{i % 4}", 16 * (i // 4))
        return idx

    def raw(self, eng, semkey, inc_n, fn, reads=(), writes=(), tag=""):
        return self._add(eng, semkey, inc_n, fn, reads, writes, tag)

    def _resolve(self, tgt_idx):
        t = self.instrs[tgt_idx]
        j = tgt_idx
        while j < len(self.instrs):
            ins = self.instrs[j]
            if ins["eng"] == t["eng"] and ins["semkey"] == t["semkey"] and ins["inc"] > 0:
                if ins["cum"] is None:
                    raise RuntimeError(f"dep on unemitted inc: {tgt_idx}:{t['tag']} -> {j}:{ins['tag']}")
                return t["semkey"], ins["cum"]
            j += 1
        raise RuntimeError(f"no covering inc for {tgt_idx}:{t['tag']} ({t['eng']}/{t['semkey']})")

    def emit(self):
        lo, hi = self.emitted, len(self.instrs)
        for i in range(lo, hi):
            ins = self.instrs[i]
            if ins["inc"] > 0:
                self._count[ins["semkey"]] = self._count.get(ins["semkey"], 0) + ins["inc"]
                ins["cum"] = self._count[ins["semkey"]]
        for i in range(lo, hi):
            ins = self.instrs[i]
            if ins["xwait"] is not None:
                ins["waits"].append(ins["xwait"])
            for d in sorted(ins["deps"]):
                td = self.instrs[d]
                if td["semkey"] == ins["eng"]:
                    if td["inc"] > 0 and td["cum"] is not None:
                        ins["waits"].append((td["semkey"], td["cum"]))
                    continue
                ins["waits"].append(self._resolve(d))
        for eng in _ENG:
            mine = [self.instrs[i] for i in range(lo, hi) if self.instrs[i]["eng"] == eng]
            if not mine:
                continue
            dec = getattr(self.block, eng)

            def body(e, mine=mine, eng=eng):
                for ins in mine:
                    for sk, val in ins["waits"]:
                        if self.hwm.get((eng, sk), -1) >= val:
                            continue
                        e.wait_ge(self.sems[sk], val)
                        self.hwm[(eng, sk)] = val
                    r = ins["fn"](e)
                    if ins["inc"] > 0:
                        r.then_inc(self.sems[ins["semkey"]], ins["inc"])
            dec(body)
        self.emitted = hi

    def barrier(self, tag="bar"):
        lasts = {}
        for ins in self.instrs:
            if ins["inc"] > 0:
                lasts[(ins["eng"], ins["semkey"])] = ins["idx"]
        dep_idxs = set(lasts.values())
        for eng in _ENG:
            i = self._add(eng, eng, 1, lambda e: e.nop(), (), (), f"{tag} {eng}")
            self.instrs[i]["deps"] |= dep_idxs

    def final_counts(self):
        return dict(self._count)


def _stage_host(inputs):
    g = lambda k: np.ascontiguousarray(np.asarray(inputs[k], dtype=np.float32))
    x = g("L0_attn_hidden_states")
    w_in = g("L0_attn_in_proj_w")
    conv_w = g("L0_attn_conv_w")
    conv_b = g("L0_attn_conv_b")
    dt_bias = g("L0_attn_dt_bias")
    A_log = g("L0_attn_A_log")
    D = g("L0_attn_D")
    norm_w = g("L0_attn_norm_w")
    w_out = g("L0_attn_out_proj_w")
    W_base = g("L0_ffn_W_base")
    b_base = g("L0_ffn_b_base")
    W_router = g("L0_ffn_W_router")
    A_stack = g("L0_ffn_A_stack")
    B_stack = g("L0_ffn_B_stack")

    sh = {}
    w_inT = np.ascontiguousarray(w_in.T)
    sh["w_in_t"] = np.ascontiguousarray(
        w_inT.reshape(NKT, 128, NCC, 128).transpose(2, 1, 0, 3)).reshape(NCC, 128, NKT * 128)
    w_outT = np.ascontiguousarray(w_out.T)
    sh["w_out_t"] = np.ascontiguousarray(
        w_outT.reshape(64, 128, 32, 128).transpose(2, 1, 0, 3)).reshape(32, 128, 64 * 128)
    w_baseT = np.ascontiguousarray(W_base.T)
    sh["w_base_t"] = np.ascontiguousarray(
        w_baseT.reshape(32, 128, 32, 128).transpose(2, 1, 0, 3)).reshape(32, 128, 32 * 128)
    sh["w_router_t"] = np.ascontiguousarray(
        W_router.T.reshape(32, 128, 8).transpose(1, 0, 2)).reshape(128, 256)
    sh["a_all_t"] = np.ascontiguousarray(
        A_stack.transpose(2, 0, 1).reshape(4096, 128).reshape(32, 128, 128)
        .transpose(1, 0, 2)).reshape(128, 4096)
    sh["b_flat"] = np.ascontiguousarray(B_stack.transpose(0, 2, 1)).reshape(128, 4096)
    sh["conv_w4"] = np.ascontiguousarray(conv_w.reshape(80, 128, 4).transpose(1, 0, 2)).reshape(128, 320)
    sh["conv_b80"] = np.ascontiguousarray(conv_b.reshape(80, 128).T)
    sh["dt_bias_c"] = dt_bias.reshape(128, 1)
    sh["a_log_c"] = A_log.reshape(128, 1)
    sh["d_pair"] = np.ascontiguousarray(
        np.concatenate([np.tile(D[0::2], (64, 1)), np.tile(D[1::2], (64, 1))], axis=0))
    sh["norm_pair"] = np.ascontiguousarray(norm_w.reshape(64, 128).T)
    sh["b_base32"] = np.ascontiguousarray(b_base.reshape(32, 128).T)
    sh["identity"] = np.eye(128, dtype=np.float32)
    sh["dmask"] = np.triu(np.ones((128, 128), dtype=np.float32))
    e8 = np.zeros((8, 128), np.float32)
    for e in range(8):
        e8[e, e * 16:(e + 1) * 16] = 1.0
    sh["e8"] = e8

    per = []
    for c in range(8):
        s, h = c // 2, c % 2
        xsT = x[s].T
        if h == 0:
            xT = np.concatenate([np.zeros((4096, HLO), np.float32), xsT[:, :T]], axis=1)
        else:
            xT = np.ascontiguousarray(xsT[:, T - HLO:])
        msel = np.zeros((128, 8), np.float32)
        if h == 1:
            msel[:, c - 1] = 1.0
        per.append(dict(xT=np.ascontiguousarray(xT), msel=msel))
    return sh, per


class MK:
    def __init__(self, nc, taps):
        self.nc = nc
        self.taps = taps
        self.tap_jobs = []
        ap = {}
        ap["xT"] = nc.dram_tensor("xT", [4096, T + HLO], F32, kind="ExternalInput").ap()
        for n, shp in [("msel", [128, 8]), ("w_router_t", [128, 256]),
                       ("a_all_t", [128, 4096]), ("b_flat", [128, 4096]),
                       ("conv_w4", [128, 320]), ("conv_b80", [128, 80]),
                       ("dt_bias_c", [128, 1]), ("a_log_c", [128, 1]),
                       ("d_pair", [128, 64]), ("norm_pair", [128, 64]), ("b_base32", [128, 32]),
                       ("identity", [128, 128]), ("dmask", [128, 128]), ("e8", [8, 128]),
                       ("w_in_t", [NCC, 128, NKT * 128]), ("w_out_t", [32, 128, 64 * 128]),
                       ("w_base_t", [32, 128, 32 * 128])]:
            ap[n] = nc.dram_tensor(n, shp, F32, kind="ExternalInput").ap()
        self.ap = ap
        self.out_t = nc.dram_tensor("out_t", [4096, T], F32, kind="ExternalOutput").ap()
        sc = {}
        sc["B"] = nc.dram_tensor("sc_B", [NG * 128, T], BF16).ap()
        sc["C"] = nc.dram_tensor("sc_C", [NG * 128, T], BF16).ap()
        sc["BT"] = nc.dram_tensor("sc_BT", [16, NG, 128, 128], BF16).ap()
        sc["xs"] = nc.dram_tensor("sc_xs", [DI, T], BF16).ap()
        sc["z"] = nc.dram_tensor("sc_z", [DI, T], BF16).ap()
        sc["xdt"] = nc.dram_tensor("sc_xdt", [16, 64, 128, 128], BF16).ap()
        sc["S"] = nc.dram_tensor("sc_S", [NCH, 128, DI], F32).ap()
        sc["yp"] = nc.dram_tensor("sc_yp", [DI, T], BF16).ap()
        sc["mo"] = nc.dram_tensor("sc_mo", [4096, T], F32).ap()
        sc["x2"] = nc.dram_tensor("sc_x2", [4096, T], F32).ap()
        sc["ffn"] = nc.dram_tensor("sc_ffn", [4096, T], F32).ap()
        sc["wob"] = nc.dram_tensor("sc_wob", [32, 128, 64 * 128], BF16).ap()
        sc["wbb"] = nc.dram_tensor("sc_wbb", [32, 128, 32 * 128], BF16).ap()
        self.sc = sc
        self.cc_in = nc.dram_tensor("cc_in", [128, DI], F32).ap()
        self.cc_out = nc.dram_tensor("cc_out", [8 * 128, DI], F32).ap()
        self.wctr = 0

    def tap_sc(self, name):
        if name in self.taps and name in self.sc:
            src = self.sc[name]
            dst = self.nc.dram_tensor("tap_" + name, list(src.shape), src.dtype,
                                      kind="ExternalOutput").ap()
            self.tap_jobs.append((name, dst, src))

    def tap_sb(self, name, src_ap, reads=()):
        if name in self.taps:
            dst = self.nc.dram_tensor("tap_" + name, list(src_ap.shape), src_ap.dtype,
                                      kind="ExternalOutput").ap()
            self.P.dma("sync", dst, src_ap, reads=reads, writes=(), tag=f"tap {name}")


    def phase0(self):
        P, cst = self.P, self.cst
        for n in cst:
            P.dma("sync", cst[n][:], self.ap[n], writes=(f"c_{n}",), tag=f"ld_{n}")
        P.op("vector", lambda e: e.memset(self.onesc[:], 1.0), writes=("ones",))
        P.op("vector", lambda e: e.memset(self.zeroc[:], 0.0), writes=("zeroc",))
        P.op("vector", lambda e: e.memset(self.onesr[0:1, :], 1.0), writes=("onesr",))
        P.op("vector", lambda e: e.memset(self.onesb[:], 1.0), writes=("onesb",))
        P.op("vector", lambda e: e.memset(self.epsc[:], EPS), writes=("ceps",))
        P.op("vector", lambda e: e.memset(self.S[:], 0.0), writes=("S",))
        P.op("scalar", lambda e: e.copy(self.idb[:], cst["identity"][:]),
             reads=("c_identity",), writes=("idb",))
        P.op("scalar", lambda e: e.activation(self.a_neg[:], cst["a_log_c"][:], AF.Exp),
             reads=("c_a_log_c",), writes=("aneg",))
        P.op("vector", lambda e: e.tensor_scalar(self.a_neg[:], self.a_neg[:], -1.0, None, ALU.mult),
             reads=("aneg",), writes=("aneg",))
        P.op("vector", lambda e: e.memset(self.dac[:, 0:128], 1.0), writes=("dac0",))

    def inproj_cc(self, cc, L, tag, halo=True):
        P = self.P
        slot = self.wctr % 2
        self.wctr += 1
        wbuf, xln, pb = L["wbuf"], L["xln"], self.pb
        W = NKT * 128 * slot
        P.dma("gpsimd", wbuf[:, W:W + NKT * 128], self.ap["w_in_t"][cc],
              reads=(), writes=(f"wb{slot}",), tag=f"{tag} wld")
        if halo:
            tiles = [(nt, nt * NT3, NT3) for nt in range(3)]
        else:
            tiles = [(nt, HLO + nt * 512, 512) for nt in range(2)]
        for nt, off, nn in tiles:
            for kt in range(NKT):
                last = kt == NKT - 1
                P.op("tensor", lambda e, nt=nt, kt=kt, W=W, off=off, nn=nn: e.matmul(
                    pb[nt][:, 0:nn], wbuf[:, W + kt * 128:W + (kt + 1) * 128],
                    xln[:, kt * THF + off:kt * THF + off + nn],
                    start=(kt == 0), stop=last, skip_group_check=True),
                    reads=(f"wb{slot}", f"xln{kt}"),
                    writes=((f"pb{nt}",) if (kt == 0 or last) else ()),
                    inc=last, tag=f"{tag} mm{nt}.{kt}")

    def ln1(self, th, L):
        P, ap, cst = self.P, self.ap, self.cst
        xtf, xln, scr2, dtb = L["xtf"], L["xln"], L["scr2"], L["dtb"]
        pb = self.pb
        m1, rstd1, onesc, onesr = self.m1, self.rstd1, self.onesc, self.onesr
        c0 = th * TH
        sA = lambda: scr2[:, 0:THF]
        sB = lambda: scr2[:, THF:2 * THF]
        for kt in range(NKT):
            P.dma("sync", xtf[:], ap["xT"][kt * 128:(kt + 1) * 128, c0:c0 + THF],
                  writes=("xtf0", "raw0"), tag=f"ln1.{th} ld{kt}")
            for nt in range(3):
                fl = kt == 0
                la = kt == NKT - 1
                P.op("tensor", lambda e, nt=nt, fl=fl, la=la: e.matmul(
                    pb[nt][0:1, 0:NT3], onesc[:], xtf[:, nt * NT3:(nt + 1) * NT3],
                    start=fl, stop=la, skip_group_check=True),
                    reads=("xtf0", "ones"), writes=((f"pb{nt}",) if (fl or la) else ()),
                    inc=True, tag=f"ln1.{th} sum{kt}.{nt}")
            sq = sA() if kt % 2 == 0 else sB()
            sqr = "scrA" if kt % 2 == 0 else "scrB"
            P.op("scalar", lambda e, sq=sq: e.activation(sq, xtf[:], AF.Square),
                 reads=("xtf0",), writes=(sqr,), tag=f"ln1.{th} sq{kt}")
            P.op("vector", lambda e, kt=kt: e.tensor_copy(
                xln[:, kt * THF:(kt + 1) * THF], xtf[:]),
                reads=("xtf0",), writes=(f"xln{kt}",), tag=f"ln1.{th} raw{kt}")
            for nt in range(3):
                fl = kt == 0
                la = kt == NKT - 1
                P.op("tensor", lambda e, nt=nt, fl=fl, la=la, sq=sq: e.matmul(
                    pb[3 + nt][0:1, 0:NT3], onesc[:], sq[:, nt * NT3:(nt + 1) * NT3],
                    start=fl, stop=la, skip_group_check=True),
                    reads=(sqr, "ones"), writes=((f"pb{3+nt}",) if (fl or la) else ()),
                    inc=True, tag=f"ln1.{th} ssq{kt}.{nt}")
        if STOP_AFTER == "ln1s":
            return
        mo = th * THF
        r1 = dtb[0:1, THF:2 * THF]
        r2 = dtb[0:1, 2 * THF:3 * THF]
        for nt in range(3):
            P.op("vector", lambda e, nt=nt: e.tensor_scalar(
                m1[0:1, mo + nt * NT3:mo + (nt + 1) * NT3], pb[nt][0:1, 0:NT3],
                1.0 / DM, None, ALU.mult),
                reads=(f"pb{nt}",), writes=("m1",), tag=f"ln1.{th} mean{nt}")
            P.op("vector", lambda e, nt=nt: e.tensor_scalar(
                r1[0:1, nt * NT3:(nt + 1) * NT3], pb[3 + nt][0:1, 0:NT3],
                1.0 / DM, None, ALU.mult),
                reads=(f"pb{3+nt}",), writes=("dt1",), tag=f"ln1.{th} msq{nt}")
        P.op("vector", lambda e: e.tensor_tensor(r2, m1[0:1, mo:mo + THF], m1[0:1, mo:mo + THF],
                                                 ALU.mult),
             reads=("m1",), writes=("dt2",), tag=f"ln1.{th} m^2")
        P.op("vector", lambda e: e.tensor_tensor(r1, r1, r2, ALU.subtract),
             reads=("dt1", "dt2"), writes=("dt1",), tag=f"ln1.{th} var")
        P.op("scalar", lambda e: e.activation(r2, r1, AF.Ln, bias=self.epsc[0:1, 0:1]),
             reads=("dt1", "ceps"), writes=("dt2",), tag=f"ln1.{th} ln")
        P.op("scalar", lambda e: e.activation(rstd1[0:1, mo:mo + THF], r2, AF.Exp, scale=-0.5),
             reads=("dt2",), writes=("rstd1",), tag=f"ln1.{th} rstd")
        if STOP_AFTER == "ln1r":
            return
        for nt in range(3):
            bk = nt % 2 * 6
            P.op("tensor", lambda e, nt=nt, bk=bk: e.matmul(
                pb[bk][:, 0:NT3], onesr[0:1, 0:128],
                m1[0:1, mo + nt * NT3:mo + (nt + 1) * NT3], start=True, stop=True),
                reads=("m1", "onesr"), writes=(f"pb{bk}",), tag=f"ln1.{th} mb{nt}")
            P.op("vector", lambda e, nt=nt, bk=bk: e.tensor_copy(scr2[:, nt * NT3:(nt + 1) * NT3],
                                                                 pb[bk][:, 0:NT3]),
                 reads=(f"pb{bk}",), writes=("scrA",), tag=f"ln1.{th} mbe{nt}")
            bk2 = (nt + 1) % 2 * 6
            P.op("tensor", lambda e, nt=nt, bk2=bk2: e.matmul(
                pb[bk2][:, 0:NT3], onesr[0:1, 0:128],
                rstd1[0:1, mo + nt * NT3:mo + (nt + 1) * NT3], start=True, stop=True),
                reads=("rstd1", "onesr"), writes=(f"pb{bk2}",), tag=f"ln1.{th} rb{nt}")
            P.op("vector", lambda e, nt=nt, bk2=bk2: e.tensor_copy(
                scr2[:, THF + nt * NT3:THF + (nt + 1) * NT3], pb[bk2][:, 0:NT3]),
                 reads=(f"pb{bk2}",), writes=("scrB",), tag=f"ln1.{th} rbe{nt}")
        if STOP_AFTER == "ln1c":
            return
        for kt in range(NKT):
            P.op("vector", lambda e, kt=kt: e.tensor_tensor(
                xln[:, kt * THF:(kt + 1) * THF], xln[:, kt * THF:(kt + 1) * THF],
                scr2[:, 0:THF], ALU.subtract),
                reads=(f"xln{kt}", "scrA"), writes=(f"xln{kt}",), tag=f"ln1b.{th} sub{kt}")
            P.op("vector", lambda e, kt=kt: e.tensor_tensor(
                xln[:, kt * THF:(kt + 1) * THF], xln[:, kt * THF:(kt + 1) * THF],
                scr2[:, THF:2 * THF], ALU.mult),
                reads=(f"xln{kt}", "scrB"), writes=(f"xln{kt}",), tag=f"ln1b.{th} mul{kt}")

    def conv_evac(self, L, tag, rs=0):
        P = self.P
        rw = L["rawb"] if rs == 0 else L["rawc"]
        wr = ("raw0", "xtf0") if rs == 0 else ("raw1", "dt2")
        for nt in range(3):
            P.op("vector", lambda e, nt=nt, rw=rw: e.tensor_copy(
                rw[:, nt * NT3:(nt + 1) * NT3], self.pb[nt][:, 0:NT3]),
                reads=(f"pb{nt}",), writes=wr, tag=f"{tag} ev{nt}")

    def conv_taps(self, cbc, L, tag, out_bf, out_res, rs=0):
        P, cst = self.P, self.cst
        rawb = L["rawb"] if rs == 0 else L["rawc"]
        rr = f"raw{rs}"
        scr2 = L["scr2"]
        cvB = L["cvb"]
        P.op("vector", lambda e, rawb=rawb: e.tensor_scalar(
            scr2[:, 0:TH], rawb[:, 5:5 + TH],
            cst["conv_w4"][:, cbc * 4:cbc * 4 + 1], None, ALU.mult),
            reads=(rr, "c_conv_w4"), writes=("scrA",), tag=f"{tag} tap0")
        P.op("vector", lambda e, rawb=rawb: e.scalar_tensor_tensor(
            scr2[:, 0:TH], rawb[:, 6:6 + TH],
            cst["conv_w4"][:, cbc * 4 + 1:cbc * 4 + 2],
            scr2[:, 0:TH], ALU.mult, ALU.add),
            reads=(rr, "scrA", "c_conv_w4"), writes=("scrA",), tag=f"{tag} tap1")
        cvC = L["cvc"]
        w2b = cst["conv_w4"][:, cbc * 4 + 2:cbc * 4 + 3].broadcast_to([128, 1, TH])
        w3b = cst["conv_w4"][:, cbc * 4 + 3:cbc * 4 + 4].broadcast_to([128, 1, TH])
        P.op("gpsimd", lambda e, w3b=w3b, rawb=rawb: e.tensor_tensor(
            cvC[:], rawb[:, 8:8 + TH], w3b, ALU.mult),
            reads=(rr, "c_conv_w4"), writes=("cvC", "dt1"), tag=f"{tag} tap3")
        P.op("gpsimd", lambda e, w2b=w2b, rawb=rawb: e.tensor_tensor(
            cvB[:], rawb[:, 7:7 + TH], w2b, ALU.mult),
            reads=(rr, "c_conv_w4"), writes=("cvB", "dt0"), tag=f"{tag} tap2")
        P.op("gpsimd", lambda e: e.tensor_tensor(cvB[:], cvB[:], cvC[:], ALU.add),
             reads=("cvB", "cvC"), writes=("cvB", "dt0"), tag=f"{tag} tap23")
        P.op("vector", lambda e: e.tensor_tensor(scr2[:, THF:THF + TH], scr2[:, 0:TH],
                                                 cvB[:], ALU.add),
             reads=("scrA", "cvB"), writes=("scrB",), tag=f"{tag} tapm")
        P.op("scalar", lambda e: e.activation(out_bf, scr2[:, THF:THF + TH], AF.Silu,
                                              bias=cst["conv_b80"][:, cbc:cbc + 1]),
             reads=("scrB", "c_conv_b80"), writes=(out_res,), tag=f"{tag} silu")

    def conv_silu(self, cbc, L, tag, out_bf, out_res):
        self.conv_evac(L, tag, 0)
        self.conv_taps(cbc, L, tag, out_bf, out_res, 0)

    def dt_phase(self, th, L):
        P, cst = self.P, self.cst
        dtb, dtT, ddT = L["dtb"], L["dtT"], L["ddT"]
        od, a_cs, dac, odlbc = self.od, self.a_cs, self.dac, self.odlbc
        pb = self.pb
        idf = cst["identity"]
        onesr = self.onesr
        B = THF
        s0 = dtb[:, 0:THF]
        s1 = dtb[:, B:2 * B]
        s2 = dtb[:, 2 * B:3 * B]
        self.inproj_cc(144, L, f"dt{th}", halo=False)
        for nt in range(2):
            P.op("vector", lambda e, nt=nt: e.tensor_copy(s0[:, nt * 512:(nt + 1) * 512],
                                                          pb[nt][:, 0:512]),
                 reads=(f"pb{nt}",), writes=("dt0",), tag=f"dt{th} ev{nt}")
        P.op("vector", lambda e: e.tensor_scalar(s1[:, 0:TH], s0[:, 0:TH], cst["dt_bias_c"][:], None, ALU.add),
             reads=("dt0", "c_dt_bias_c"), writes=("dt1",), tag=f"dt{th} bias")
        P.op("scalar", lambda e: e.activation(s2[:, 0:TH], s1[:, 0:TH], AF.Abs),
             reads=("dt1",), writes=("dt2",), tag=f"dt{th} abs")
        P.op("scalar", lambda e: e.activation(s2[:, 0:TH], s2[:, 0:TH], AF.Exp, scale=-1.0),
             reads=("dt2",), writes=("dt2",), tag=f"dt{th} exp")
        P.op("scalar", lambda e: e.activation(s2[:, 0:TH], s2[:, 0:TH], AF.Ln, bias=1.0),
             reads=("dt2",), writes=("dt2",), tag=f"dt{th} ln1p")
        P.op("vector", lambda e: e.tensor_scalar(s1[:, 0:TH], s1[:, 0:TH], 0.0, None, ALU.max),
             reads=("dt1",), writes=("dt1",), tag=f"dt{th} relu")
        P.op("vector", lambda e: e.tensor_tensor(s0[:, 0:TH], s1[:, 0:TH], s2[:, 0:TH], ALU.add),
             reads=("dt1", "dt2"), writes=("dt0",), tag=f"dt{th} dt")
        P.op("vector", lambda e: e.tensor_scalar(s1[:, 0:TH], s0[:, 0:TH],
                                                 self.a_neg[:], None, ALU.mult),
             reads=("dt0", "aneg"), writes=("dt1",), tag=f"dt{th} dA")
        t0 = th * TH
        for c in range(4):
            P.op("vector", lambda e, c=c: e.tensor_tensor_scan(
                a_cs[:, t0 + c * CH:t0 + (c + 1) * CH], s1[:, c * CH:(c + 1) * CH],
                s1[:, c * CH:(c + 1) * CH], 0.0, ALU.add, ALU.bypass),
                reads=("dt1",), writes=("acs",), tag=f"dt{th} cs{c}")
        P.op("scalar", lambda e: e.activation(od[:, t0:t0 + TH], a_cs[:, t0:t0 + TH], AF.Exp),
             reads=("acs",), writes=("od",), tag=f"dt{th} od")
        for c in range(4):
            P.op("vector", lambda e, c=c: e.tensor_scalar(
                s1[:, c * CH:(c + 1) * CH], a_cs[:, t0 + c * CH:t0 + (c + 1) * CH],
                a_cs[:, t0 + (c + 1) * CH - 1:t0 + (c + 1) * CH], None, ALU.subtract),
                reads=("acs",), writes=("dt1",), tag=f"dt{th} dec{c}a")
        P.op("scalar", lambda e: e.activation(s1[:, 0:TH], s1[:, 0:TH], AF.Exp, scale=-1.0),
             reads=("dt1",), writes=("dt1",), tag=f"dt{th} dec")
        P.op("vector", lambda e: e.tensor_tensor(s2[:, 0:TH], s0[:, 0:TH],
                                                 s1[:, 0:TH], ALU.mult),
             reads=("dt0", "dt1"), writes=("dt2",), tag=f"dt{th} dtdec")
        for st in range(8):
            dst = slice(st * 128, (st + 1) * 128)
            P.op("tensor", lambda e, st=st: e.transpose(
                pb[3][:, 0:128], s0[:, st * 128:(st + 1) * 128], idf[:]),
                reads=("dt0", "c_identity"), writes=("pb3",), tag=f"dt{th} trT{st}")
            P.op("vector", lambda e, dst=dst: e.tensor_copy(dtT[:, dst], pb[3][:, 0:128]),
                 reads=("pb3",), writes=("dtT",), tag=f"dt{th} trTe{st}")
            P.op("tensor", lambda e, st=st: e.transpose(
                pb[4][:, 0:128], s2[:, st * 128:(st + 1) * 128], idf[:]),
                reads=("dt2", "c_identity"), writes=("pb4",), tag=f"dt{th} trD{st}")
            P.op("vector", lambda e, dst=dst: e.tensor_copy(ddT[:, dst], pb[4][:, 0:128]),
                 reads=("pb4",), writes=("ddT",), tag=f"dt{th} trDe{st}")
        for c in range(4):
            gc = th * 4 + c
            P.dma("sync", self.flatr[0:1, 0:128],
                  a_cs[:, t0 + (c + 1) * CH - 1:t0 + (c + 1) * CH],
                  reads=("acs",), writes=("odlrow",), tag=f"dt{th} odl{c}")
            P.op("tensor", lambda e: e.matmul(pb[3][:, 0:128], onesr[0:1, 0:128],
                                              self.flatr[0:1, 0:128], start=True, stop=True),
                 reads=("odlrow", "onesr"), writes=("pb3",), tag=f"dt{th} odlb{c}")
            P.op("scalar", lambda e, gc=gc: e.activation(odlbc[:, gc * 128:(gc + 1) * 128],
                                                         pb[3][:, 0:128], AF.Exp),
                 reads=("pb3",), writes=(f"odlbc{gc}",), tag=f"dt{th} odlbe{c}")
            if gc + 1 < NCH:
                P.op("vector", lambda e, gc=gc: e.tensor_tensor(
                    dac[:, (gc + 1) * 128:(gc + 2) * 128], dac[:, gc * 128:(gc + 1) * 128],
                    odlbc[:, gc * 128:(gc + 1) * 128], ALU.mult),
                    reads=(f"odlbc{gc}", f"dac{gc}"), writes=(f"dac{gc+1}",), tag=f"dt{th} dac{gc}")

    def group_phase(self, th, g, L):
        P, cst, sc = self.P, self.cst, self.sc
        pb, pbT = self.pb, self.pbT
        btst = L["btst"]
        dtT, ddT = L["dtT"], L["ddT"]
        S, odlbc, gtmp = self.S, self.odlbc, L["gtmp"]
        idb = self.idb
        t0 = th * TH

        def conv_bc(cc, cbc, dest_rows, dest, res, bt_too, tag):
            bcbh = L["bcb"][:, 0:TH]
            self.inproj_cc(cc, L, tag)
            self.conv_silu(cbc, L, tag, bcbh, "bcb0")
            P.dma("sync", dest[dest_rows:dest_rows + 128, t0:t0 + TH], bcbh,
                  reads=("bcb0",), writes=(res,), tag=f"{tag} st")
            if bt_too:
                for st in range(8):
                    P.op("tensor", lambda e, st=st, bcbh=bcbh: e.transpose(
                        pbT[:, st * 128:(st + 1) * 128], bcbh[:, st * 128:(st + 1) * 128], idb[:]),
                        reads=("bcb0", "idb"), writes=(("pbT",) if st in (0, 7) else ()),
                        inc=(st == 7), tag=f"{tag} btT{st}")
                P.op("vector", lambda e: e.tensor_copy(btst[:], pbT[:]),
                     reads=("pbT",), writes=("btst",), tag=f"{tag} btTe")
                P.dma("scalar", sc["BT"][th * 8:th * 8 + 8, g].rearrange("st p t -> p st t"),
                      btst[:].rearrange("p (st t) -> p st t", st=8),
                      reads=("btst",), writes=("scBT",), tag=f"{tag} btst")

        conv_bc(128 + g, 64 + g, g * 128, sc["B"], "scB", True, f"B{th}.{g}")
        conv_bc(136 + g, 72 + g, g * 128, sc["C"], "scC", False, f"C{th}.{g}")

        bcb2, xst, gxsT, gxdd = L["bcb"], L["xst"], L["gxsT"], L["gxdd"]

        def stageMV(j):
            par = j % 2
            cc = 64 + g * 8 + j
            tag = f"x{th}.{g}.{j}"
            self.inproj_cc(cc, L, tag)
            self.conv_evac(L, tag, par)

        def stageMVc(j):
            par = j % 2
            cbc = g * 8 + j
            tag = f"x{th}.{g}.{j}"
            self.conv_taps(cbc, L, tag, bcb2[:, par * TH:(par + 1) * TH], f"bcb{par}", par)
            P.dma("sync", sc["xs"][cbc * 128:(cbc + 1) * 128, t0:t0 + TH],
                  bcb2[:, par * TH:(par + 1) * TH],
                  reads=(f"bcb{par}",), writes=("scxs",), tag=f"{tag} st")

        def stageTDpe(j):
            par = j % 2
            tag = f"x{th}.{g}.{j}"
            for st in range(8):
                P.op("tensor", lambda e, st=st, par=par: e.transpose(
                    pbT[:, st * 128:(st + 1) * 128],
                    bcb2[:, par * TH + st * 128:par * TH + (st + 1) * 128], idb[:]),
                    reads=(f"bcb{par}", "idb"), writes=(("pbT",) if st in (0, 7) else ()),
                    inc=(st == 7), tag=f"{tag} xT{st}")

        def stageTDdve(j):
            par = j % 2
            cbc = g * 8 + j
            tag = f"x{th}.{g}.{j}"
            P.op("vector", lambda e, par=par: e.tensor_copy(
                gxsT[:, par * 1024:(par + 1) * 1024], pbT[:]),
                reads=("pbT",), writes=(f"gxsT{par}",), tag=f"{tag} xTe")
            xin = gxsT[:, par * 1024:(par + 1) * 1024].rearrange(
                "p (st hh pp) -> p st hh pp", st=8, hh=2)
            dAP = dtT[:, :].rearrange("p (st r) -> p st r", st=8)[:, :, 2 * cbc:2 * cbc + 2] \
                .broadcast_to([128, 8, 2, 64])
            ddAP = ddT[:, :].rearrange("p (st r) -> p st r", st=8)[:, :, 2 * cbc:2 * cbc + 2] \
                .broadcast_to([128, 8, 2, 64])
            P.op("vector", lambda e, xin=xin, dAP=dAP, par=par: e.tensor_tensor(
                xst[:, par * 1024:(par + 1) * 1024].rearrange(
                    "p (st hh pp) -> p st hh pp", st=8, hh=2),
                xin, dAP, ALU.mult),
                reads=(f"gxsT{par}", "dtT"), writes=(f"xst{par}",), tag=f"{tag} xdt")
            P.op("vector", lambda e, xin=xin, ddAP=ddAP, par=par: e.tensor_tensor(
                gxdd[:, par * 1024:(par + 1) * 1024].rearrange(
                    "p (st hh pp) -> p st hh pp", st=8, hh=2),
                xin, ddAP, ALU.mult),
                reads=(f"gxsT{par}", "ddT"), writes=(f"gxdd{par}",), tag=f"{tag} xdd")
            P.dma("scalar", sc["xdt"][th * 8:th * 8 + 8, cbc].rearrange("st p t -> p st t"),
                  xst[:, par * 1024:(par + 1) * 1024].rearrange("p (st t) -> p st t", st=8),
                  reads=(f"xst{par}",), writes=("scxdt",), tag=f"{tag} xdtst")

        sstS = L["sstS"]

        def stageST(j):
            par = j % 2
            cbc = g * 8 + j
            so = 0
            tag = f"x{th}.{g}.{j}"
            for c in range(4):
                gc = th * 4 + c
                csl = slice(cbc * 128, (cbc + 1) * 128)
                P.op("vector", lambda e, c=c, csl=csl, so=so: e.tensor_copy(
                    sstS[:, so + c * 128:so + (c + 1) * 128], S[:, csl]),
                    reads=("S",), writes=("sstS",), tag=f"{tag} Scp{c}")
                bank = 5 if c % 2 == 0 else 6
                n_mm = 0
                for sti in range(2):
                    st = 2 * c + sti
                    for hh in range(2):
                        first = n_mm == 0
                        last = n_mm == 3
                        P.op("tensor", lambda e, st=st, hh=hh, bank=bank, first=first,
                             last=last, par=par: e.matmul(
                            pb[bank][:, hh * 64:hh * 64 + 64],
                            btst[:, st * 128:(st + 1) * 128],
                            gxdd[:, par * 1024 + st * 128 + hh * 64:par * 1024 + st * 128 + hh * 64 + 64],
                            start=first, stop=last, skip_group_check=True),
                            reads=("btst", f"gxdd{par}"),
                            writes=((f"pb{bank}",) if (first or last) else ()),
                            inc=last, tag=f"{tag} Smm{c}.{n_mm}")
                        n_mm += 1
                obs = odlbc[:, gc * 128 + 2 * cbc:gc * 128 + 2 * cbc + 2]
                P.op("vector", lambda e, csl=csl, obs=obs: e.tensor_tensor(
                    gtmp[:, 0:128], S[:, csl], obs.broadcast_to([128, 2, 64]), ALU.mult),
                    reads=("S", f"odlbc{gc}"), writes=("gtmp",), tag=f"{tag} Sdec{c}")
                P.op("vector", lambda e, csl=csl, bank=bank: e.tensor_tensor(
                    S[:, csl], gtmp[:, 0:128], pb[bank][:, 0:128], ALU.add),
                    reads=("gtmp", f"pb{bank}"), writes=("S",), tag=f"{tag} Supd{c}")
            csl = slice(cbc * 128, (cbc + 1) * 128)
            P.dma("scalar", sc["S"][th * 4:th * 4 + 4, :, csl].rearrange("c p t -> p c t"),
                  sstS[:, so:so + 512].rearrange("p (c t) -> p c t", c=4),
                  reads=("sstS",), writes=("scS",), tag=f"{tag} Sst")

        for j in range(8):
            if j >= 2:
                stageTDdve(j - 2)
                stageST(j - 2)
            stageMV(j)
            if j >= 1:
                stageMVc(j - 1)
                stageTDpe(j - 1)
        stageMVc(7)
        stageTDdve(6)
        stageST(6)
        stageTDpe(7)
        stageTDdve(7)
        stageST(7)

    def z_phase(self, th, L):
        P, sc = self.P, self.sc
        pb = self.pb
        t0 = th * TH
        for cc in range(64):
            par = cc % 2
            zb = L["bcb"][:, par * TH:(par + 1) * TH]
            tag = f"z{th}.{cc}"
            self.inproj_cc(cc, L, tag, halo=False)
            for nt in range(2):
                P.op("scalar", lambda e, nt=nt, zb=zb: e.activation(zb[:, nt * 512:(nt + 1) * 512],
                                                                    pb[nt][:, 0:512], AF.Silu),
                     reads=(f"pb{nt}",), writes=(f"bcb{par}",), tag=f"{tag} ev{nt}")
            P.dma("sync", sc["z"][cc * 128:(cc + 1) * 128, t0:t0 + TH], zb,
                  reads=(f"bcb{par}",), writes=("scz",), tag=f"{tag} st")

    def subpass1(self):
        nc, P = self.nc, self.P
        with nc.sbuf_tensor("xln", [128, NKT * THF], BF16) as xln, \
             nc.sbuf_tensor("scr2", [128, 2 * THF], F32) as scr2, \
             nc.sbuf_tensor("wbuf", [128, 2 * NKT * 128], BF16) as wbuf, \
             nc.sbuf_tensor("rawb", [128, THF], F32) as rawb, \
             nc.sbuf_tensor("bcb", [128, 2 * TH], BF16) as bcb, \
             nc.sbuf_tensor("xst", [128, 2048], BF16) as xst, \
             nc.sbuf_tensor("btst", [128, 8 * 128], BF16) as btst, \
             nc.sbuf_tensor("gxsT", [128, 2048], BF16) as gxsT, \
             nc.sbuf_tensor("gxdd", [128, 2048], BF16) as gxdd, \
             nc.sbuf_tensor("dtb", [128, 3 * THF], F32) as dtb, \
             nc.sbuf_tensor("dtT", [128, TH], F32) as dtT, \
             nc.sbuf_tensor("ddT", [128, TH], F32) as ddT, \
             nc.sbuf_tensor("gtmp", [128, 256], F32) as gtmp, \
             nc.sbuf_tensor("sstS", [128, 512], F32) as sstS:
            xtf = rawb
            cvb = dtb[:, 0:TH]
            cvc = dtb[:, THF:THF + TH]
            rawc = dtb[:, 2 * THF:3 * THF]
            L = dict(xln=xln, xtf=xtf, scr2=scr2, wbuf=wbuf, rawb=rawb, rawc=rawc, bcb=bcb,
                     xst=xst, btst=btst, gxsT=gxsT, gxdd=gxdd, dtb=dtb, dtT=dtT, ddT=ddT,
                     gtmp=gtmp, cvb=cvb, cvc=cvc, sstS=sstS)
            def stop(name):
                return STOP_AFTER == name
            done = False
            for th in range(2):
                self.ln1(th, L)
                if stop(f"ln1.{th}") or STOP_AFTER in ("ln1s", "ln1r", "ln1c"):
                    done = True; break
                self.dt_phase(th, L)
                if stop(f"dt.{th}"):
                    done = True; break
                for g in range(NG):
                    self.group_phase(th, g, L)
                    if stop(f"g{g}.{th}"):
                        done = True; break
                if done:
                    break
                self.z_phase(th, L)
                if stop(f"sub1.{th}"):
                    done = True; break
            self.tap_sb("xln", xln[:], tuple(f"xln{k}" for k in range(NKT)))
            self.tap_sb("dtT", dtT[:], ("dtT",))
            self.tap_sb("ddT", ddT[:], ("ddT",))
            self.tap_sb("acs", self.a_cs[:], ("acs",))
            self.tap_sb("od", self.od[:], ("od",))
            self.tap_sb("Sfin", self.S[:], ("S",))
            self.tap_sb("dac", self.dac[:], tuple(f"dac{i}" for i in range(NCH)))
            self.tap_sb("odlbc", self.odlbc[:], tuple(f"odlbc{i}" for i in range(NCH)))
            self.tap_sb("m1", self.m1[:], ("m1",))
            self.tap_sb("rstd1", self.rstd1[:], ("rstd1",))
            P.emit()

    def build(self):
        nc = self.nc
        from contextlib import ExitStack
        with nc.Block() as block:
            self.P = Plan(nc, block)
            P = self.P
            P.make_sems()
            with ExitStack() as stk:
                sb = lambda n, shp, dt: stk.enter_context(nc.sbuf_tensor(n, shp, dt))
                ps = lambda n, shp, dt: stk.enter_context(nc.psum_tensor(n, shp, dt))
                self.cst = {}
                for n, shp in [("conv_w4", [128, 320]), ("conv_b80", [128, 80]),
                               ("dt_bias_c", [128, 1]), ("a_log_c", [128, 1]),
                               ("d_pair", [128, 64]), ("norm_pair", [128, 64]),
                               ("b_base32", [128, 32]), ("identity", [128, 128]),
                               ("dmask", [128, 128]), ("msel", [128, 8])]:
                    self.cst[n] = sb("c_" + n, shp, F32)
                self.cst["e8"] = sb("c_e8", [8, 128], F32)
                self.idb = sb("c_idb", [128, 128], BF16)
                self.onesc = sb("c_ones", [128, 1], F32)
                self.onesr = sb("c_onesr", [1, 128], F32)
                self.flatr = sb("c_flatr", [1, 128], F32)
                self.onesb = sb("c_onesb", [1, 128], BF16)
                self.a_neg = sb("c_aneg", [128, 1], F32)
                self.epsc = sb("c_epsc", [128, 1], F32)
                self.zeroc = sb("c_zeroc", [128, 1], F32)
                self.m1 = sb("g_m1", [1, 2 * THF], F32)
                self.rstd1 = sb("g_rstd1", [1, 2 * THF], F32)
                self.pb = [ps(f"pb{i}", [128, 512], F32) for i in range(7)]
                self.pbT = ps("pbT", [128, 1024], BF16)
                mstk = ExitStack()
                odstk = ExitStack()
                try:
                    msb = lambda n, shp, dt: mstk.enter_context(nc.sbuf_tensor(n, shp, dt))
                    osb = lambda n, shp, dt: odstk.enter_context(nc.sbuf_tensor(n, shp, dt))
                    self.S = msb("g_S", [128, DI], F32)
                    self.od = osb("g_od", [128, T], BF16)
                    self.a_cs = osb("g_acs", [128, T], F32)
                    self.dac = osb("g_dac", [128, NCH * 128], F32)
                    self.odlbc = osb("g_odlbc", [128, NCH * 128], F32)
                    self.phase0()
                    P.emit()
                    if STOP_AFTER != "p0":
                        self.subpass1()
                    if STOP_AFTER is None or STOP_AFTER in ("ssd", "op", "ffn"):
                        self.exchange_and_sub2()
                        self.rrms = self.S[0:1, 0:2048]
                        self.m3R = self.S[0:1, 2048:4096]
                        self.r3R = self.S[0:1, 4096:6144]
                        self.sub2_ssd()
                        odstk.close()
                        if STOP_AFTER != "ssd":
                            self.sub2_rest()
                    for name, dst, src in self.tap_jobs:
                        P.dma("gpsimd", dst, src, reads=("sc" + name,), tag=f"tap {name}")
                    P.emit()
                finally:
                    for stk in (odstk, mstk):
                        try:
                            stk.close()
                        except Exception:
                            pass
                fc = P.final_counts()

                def drain(e, fc=fc):
                    for k, v in fc.items():
                        e.wait_ge(P.sems[k], v)
                block.sync(drain)
        return nc

    def exchange_and_sub2(self):
        nc, P, sc, cst = self.nc, self.P, self.sc, self.cst
        pb, pbT = self.pb, self.pbT
        from contextlib import ExitStack
        P.dma("sync", self.cc_in, self.S[:], reads=("S",), writes=("ccin",), tag="cc in")
        if SKIP_CC:
            for r in range(8):
                P.dma("gpsimd", self.cc_out[r * 128:(r + 1) * 128, :], self.cc_in,
                      reads=("ccin",), writes=("ccout",), tag=f"cc fake{r}")
        else:
            P.raw("gpsimd", "cc", 1, lambda e: e.collective_compute(
                "AllGather", ALU.bypass, replica_groups=[[0, 1, 2, 3, 4, 5, 6, 7]],
                ins=[self.cc_in.opt()], outs=[self.cc_out.opt()]),
                reads=("ccin",), writes=("ccout",), tag="allgather")
        P.emit()

    def ssd_phase(self, L):
        nc, P, sc, cst = self.nc, self.P, self.sc, self.cst
        pb = self.pb
        S0, sstg = L["S0"], L["sstg"]
        gB, gC2, gxdt2 = L["gB"], L["gC2"], L["gxdt2"]
        gxs2, gz2, SfG, Sfb = L["gxs2"], L["gz2"], L["SfG"], L["Sfb"]
        k2a, k2b, odf, l3, m3, Gm2 = L["k2a"], L["k2b"], L["odf"], L["l3"], L["m3"], L["Gm2"]
        cod, yst, ygb, ypb, ysq, acc = L["cod"], L["yst"], L["ygb"], L["ypb"], L["ysq"], L["acc"]
        rr2 = L["rr2"]
        rrms = self.rrms
        onesc, onesb = self.onesc, self.onesb
        dmask = cst["dmask"]

        def preload(ci, gi):
            if ci >= NCH:
                return
            sl = gi % 2
            t0c = ci * 256
            g = gi
            P.dma("sync", gB[:, sl * 256:(sl + 1) * 256],
                  sc["B"][g * 128:(g + 1) * 128, t0c:t0c + 256],
                  reads=("scB",), writes=(f"gB{sl}",), tag=f"Y{ci}.{g} Bld")
            for hv in range(2):
                P.dma("sync", gC2[:, sl * 512 + hv * 256:sl * 512 + (hv + 1) * 256],
                      sc["C"][g * 128:(g + 1) * 128, t0c:t0c + 256],
                      reads=("scC",), writes=(f"gC{sl}",), tag=f"Y{ci}.{g} Cld{hv}")
            for i in range(2):
                P.dma("sync", gxdt2[:, sl * 2048 + i * 1024:sl * 2048 + (i + 1) * 1024]
                      .rearrange("p (j t) -> p j t", j=8),
                      sc["xdt"][2 * ci + i, g * 8:(g + 1) * 8].rearrange("j p t -> p j t"),
                      reads=("scxdt",), writes=(f"gxdt{sl}",), tag=f"Y{ci}.{g} xdtld{i}")
            P.dma("sync", gxs2[:, sl * 2048:(sl + 1) * 2048].rearrange("p (pt t) -> p pt t", pt=8),
                  sc["xs"][g * 1024:(g + 1) * 1024, t0c:t0c + 256]
                  .rearrange("(pt p) t -> p pt t", p=128),
                  reads=("scxs",), writes=(f"gxs{sl}",), tag=f"Y{ci}.{g} xsld")
            P.dma("sync", gz2[:, sl * 2048:(sl + 1) * 2048].rearrange("p (pt t) -> p pt t", pt=8),
                  sc["z"][g * 1024:(g + 1) * 1024, t0c:t0c + 256]
                  .rearrange("(pt p) t -> p pt t", p=128),
                  reads=("scz",), writes=(f"gz{sl}",), tag=f"Y{ci}.{g} zld")
            P.dma("sync", SfG[:, sl * 1024:(sl + 1) * 1024],
                  sc["S"][ci][:, g * 1024:(g + 1) * 1024],
                  reads=("scS",), writes=(f"SfG{sl}",), tag=f"Y{ci}.{g} Sld")

        preload(0, 0)
        preload(0, 1)
        for c in range(NCH):
            t0c = c * 256
            for g in range(NG):
                sl = g % 2
                gsl = slice(g * 1024, (g + 1) * 1024)
                Ssl = slice(sl * 1024, (sl + 1) * 1024)
                dacAP = self.dac[:, c * 128 + g * 16:c * 128 + (g + 1) * 16] \
                    .broadcast_to([128, 16, 64])
                P.dma("sync", k2a[1:2, :], self.a_cs[g * 16:(g + 1) * 16, t0c:t0c + 256],
                      reads=("acs",), writes=("k2a1",), tag=f"Y{c}.{g} k2a")
                P.dma("sync", k2b[0:1, :], self.a_cs[g * 16:(g + 1) * 16, t0c:t0c + 256],
                      reads=("acs",), writes=("k2b0",), tag=f"Y{c}.{g} k2b")
                P.dma("sync", odf[0:1, :], self.od[g * 16:(g + 1) * 16, t0c:t0c + 256],
                      reads=("od",), writes=("odf", "r3R"), tag=f"Y{c}.{g} odf")
                P.op("vector", lambda e, gsl=gsl, dacAP=dacAP: e.tensor_tensor(
                    sstg[:], S0[:, gsl], dacAP, ALU.mult),
                    reads=("S0", f"dac{c}"), writes=("sstg",), tag=f"Y{c}.{g} Sdac")
                P.op("vector", lambda e, sl=sl, Ssl=Ssl: e.tensor_tensor(
                    Sfb[:, Ssl], SfG[:, Ssl], sstg[:], ALU.add),
                    reads=(f"SfG{sl}", "sstg"), writes=(f"Sfb{sl}",), tag=f"Y{c}.{g} Sfb")
                P.op("tensor", lambda e, sl=sl: e.matmul(
                    pb[5][:, 0:256], gB[:, sl * 256:sl * 256 + 128],
                    gC2[:, sl * 512:sl * 512 + 256],
                    start=True, stop=False, skip_group_check=True),
                    reads=(f"gB{sl}", f"gC{sl}"), writes=("pb5",), tag=f"Y{c}.{g} G1")
                P.op("tensor", lambda e, sl=sl: e.matmul(
                    pb[5][:, 256:384], gB[:, sl * 256 + 128:sl * 256 + 256],
                    gC2[:, sl * 512 + 128:sl * 512 + 256],
                    start=False, stop=True, skip_group_check=True),
                    reads=(f"gB{sl}", f"gC{sl}"), writes=("pb5",), tag=f"Y{c}.{g} G2")
                P.op("vector", lambda e: e.tensor_tensor(Gm2[:, 0:128], pb[5][:, 0:128],
                                                         dmask[:], ALU.mult),
                     reads=("pb5", "c_dmask"), writes=("Gm",), tag=f"Y{c}.{g} Gm0")
                P.op("vector", lambda e: e.tensor_copy(Gm2[:, 128:256], pb[5][:, 128:256]),
                     reads=("pb5",), writes=("Gm",), tag=f"Y{c}.{g} Gmm")
                P.op("vector", lambda e: e.tensor_tensor(Gm2[:, 256:384], pb[5][:, 256:384],
                                                         dmask[:], ALU.mult),
                     reads=("pb5", "c_dmask"), writes=("Gm",), tag=f"Y{c}.{g} Gm1")
                P.op("vector", lambda e: e.tensor_copy(Gm2[:, 384:768], Gm2[:, 0:384]),
                     reads=("Gm",), writes=("Gm",), tag=f"Y{c}.{g} Gmd")

                def stageA(pt):
                    ms = (pt % 2) * 768
                    cs = (pt % 2) * 512
                    kb = pt * 512
                    for par, db in ((0, 6), (1, 4)):
                        po = kb + par * 256
                        P.op("tensor", lambda e, po=po, db=db: e.matmul(
                            pb[db][:, 0:256], k2a[0:2, po:po + 128], k2b[0:2, po:po + 256],
                            start=True, stop=False, skip_group_check=True),
                            reads=("k2a1", "k2b0", "k2ac", "k2bc"), writes=(f"pb{db}",),
                            inc=False, tag=f"Y{c}.{g}.{pt} d{par}a")
                        P.op("tensor", lambda e, po=po, db=db: e.matmul(
                            pb[db][:, 256:384], k2a[0:2, po + 128:po + 256],
                            k2b[0:2, po + 128:po + 256],
                            start=False, stop=True, skip_group_check=True),
                            reads=("k2a1", "k2b0", "k2ac", "k2bc"), writes=(f"pb{db}",),
                            inc=True, tag=f"Y{c}.{g}.{pt} d{par}b")
                    P.op("tensor", lambda e, kb=kb: e.matmul(
                        pb[3][:, 0:512], onesb[0:1, 0:128], odf[0:1, kb:kb + 512],
                        start=True, stop=True),
                        reads=("odf", "onesb"), writes=("pb3",), tag=f"Y{c}.{g}.{pt} odb")
                    P.op("scalar", lambda e: e.activation(l3[:, 0:384], pb[6][:, 0:384], AF.Abs),
                         reads=("pb6",), writes=("l3",), tag=f"Y{c}.{g}.{pt} abs0")
                    P.op("scalar", lambda e: e.activation(l3[:, 384:768], pb[4][:, 0:384], AF.Abs),
                         reads=("pb4",), writes=("l3",), tag=f"Y{c}.{g}.{pt} abs1")
                    P.op("scalar", lambda e: e.activation(l3[:, 0:768], l3[:, 0:768],
                                                          AF.Exp, scale=-1.0),
                         reads=("l3",), writes=("l3",), tag=f"Y{c}.{g}.{pt} exp")
                    P.op("vector", lambda e, ms=ms: e.tensor_tensor(
                        m3[:, ms:ms + 768], Gm2[:, 0:768], l3[:, 0:768], ALU.mult),
                        reads=("Gm", "l3"), writes=(f"m3{pt%2}",), tag=f"Y{c}.{g}.{pt} m")
                    P.op("vector", lambda e, cs=cs, sl=sl: e.tensor_tensor(
                        cod[:, cs:cs + 512], gC2[:, sl * 512:(sl + 1) * 512], pb[3][:, 0:512],
                        ALU.mult),
                        reads=(f"gC{sl}", "pb3"), writes=(f"cod{pt%2}",), tag=f"Y{c}.{g}.{pt} cod")

                def stageB(pt):
                    ms = (pt % 2) * 768
                    cs = (pt % 2) * 512
                    yb = pt % 2
                    ps = (pt % 2) * 256
                    col = g * 8 + pt
                    xb = sl * 2048 + pt * 128
                    for par in range(2):
                        rsl = slice(par * 64, par * 64 + 64)
                        mo3 = ms + par * 384
                        hh = 2 * pt + par
                        P.op("tensor", lambda e, rsl=rsl, yb=yb, mo3=mo3, xb=xb, par=par: e.matmul(
                            pb[yb][rsl, 0:256], gxdt2[:, xb + par * 64:xb + par * 64 + 64],
                            m3[:, mo3:mo3 + 256],
                            start=True, stop=False, skip_group_check=True),
                            reads=(f"gxdt{sl}", f"m3{pt%2}"), writes=(f"pb{yb}",),
                            inc=False, tag=f"Y{c}.{g}.{pt}.{par} Y1")
                        P.op("tensor", lambda e, rsl=rsl, yb=yb, mo3=mo3, xb=xb, par=par: e.matmul(
                            pb[yb][rsl, 128:256],
                            gxdt2[:, 1024 + xb + par * 64:1024 + xb + par * 64 + 64],
                            m3[:, mo3 + 256:mo3 + 384],
                            start=False, stop=False, skip_group_check=True),
                            reads=(f"gxdt{sl}", f"m3{pt%2}"),
                            inc=False, tag=f"Y{c}.{g}.{pt}.{par} Y2")
                        P.op("tensor", lambda e, rsl=rsl, yb=yb, hh=hh, cs=cs, par=par, sl=sl: e.matmul(
                            pb[yb][rsl, 0:256], Sfb[:, sl * 1024 + hh * 64:sl * 1024 + (hh + 1) * 64],
                            cod[:, cs + par * 256:cs + (par + 1) * 256],
                            start=False, stop=True, skip_group_check=True),
                            reads=(f"Sfb{sl}", f"cod{pt%2}"), writes=(f"pb{yb}",),
                            inc=(par == 1), tag=f"Y{c}.{g}.{pt}.{par} Y3")
                    P.op("vector", lambda e, pt=pt, col=col, yb=yb, sl=sl: e.scalar_tensor_tensor(
                        yst[:], gxs2[:, sl * 2048 + pt * 256:sl * 2048 + (pt + 1) * 256],
                        cst["d_pair"][:, col:col + 1],
                        pb[yb][:, 0:256], ALU.mult, ALU.add),
                        reads=(f"gxs{sl}", "c_d_pair", f"pb{yb}"), writes=("yst",),
                        tag=f"Y{c}.{g}.{pt} ev")
                    P.op("vector", lambda e, pt=pt, sl=sl: e.tensor_tensor(
                        ygb[:], yst[:], gz2[:, sl * 2048 + pt * 256:sl * 2048 + (pt + 1) * 256],
                        ALU.mult),
                        reads=("yst", f"gz{sl}"), writes=("ygb",), tag=f"Y{c}.{g}.{pt} yg")
                    P.op("vector", lambda e, col=col, ps=ps: e.tensor_scalar(
                        ypb[:, ps:ps + 256], ygb[:],
                        cst["norm_pair"][:, col:col + 1], None, ALU.mult),
                        reads=("ygb", "c_norm_pair"), writes=(f"ypb{pt%2}",),
                        tag=f"Y{c}.{g}.{pt} yp")
                    P.dma("gpsimd", sc["yp"][col * 128:(col + 1) * 128, c * 256:(c + 1) * 256],
                          ypb[:, ps:ps + 256], reads=(f"ypb{pt%2}",),
                          writes=("scyp",), tag=f"Y{c}.{g}.{pt} st")
                    P.op("scalar", lambda e, ps=ps: e.activation(
                        ysq[:, ps:ps + 256], ygb[:], AF.Square),
                        reads=("ygb",), writes=(f"ysq{pt%2}",), tag=f"Y{c}.{g}.{pt} sq")
                    if g == 0 and pt == 0:
                        P.op("gpsimd", lambda e, ps=ps: e.tensor_copy(acc[:], ysq[:, ps:ps + 256]),
                             reads=(f"ysq{pt%2}",), writes=("acc",), tag=f"Y{c}.{g}.{pt} acc")
                    else:
                        P.op("gpsimd", lambda e, ps=ps: e.tensor_tensor(
                            acc[:], acc[:], ysq[:, ps:ps + 256], ALU.add),
                            reads=("acc", f"ysq{pt%2}"), writes=("acc",), tag=f"Y{c}.{g}.{pt} acc")

                for pt in range(8):
                    stageA(pt)
                    if pt >= 1:
                        stageB(pt - 1)
                stageB(7)
                preload(c + (g + 2) // 8, (g + 2) % 8)
            P.op("tensor", lambda e: e.matmul(pb[2][0:1, 0:256], onesc[:], acc[:],
                                              start=True, stop=True, skip_group_check=True),
                 reads=("acc", "ones"), writes=("pb2",), tag=f"Y{c} rmsmm")
            P.op("vector", lambda e: e.tensor_scalar(rr2[0:1, 0:256], pb[2][0:1, 0:256],
                                                     1.0 / DI, None, ALU.mult),
                 reads=("pb2",), writes=("rr2", "m3R"), tag=f"Y{c} rmean")
            P.op("scalar", lambda e: e.activation(rr2[0:1, 256:512], rr2[0:1, 0:256], AF.Ln,
                                                  bias=self.epsc[0:1, 0:1]),
                 reads=("rr2", "ceps"), writes=("rr2b", "m3R"), tag=f"Y{c} rln")
            P.op("scalar", lambda e, c=c: e.activation(rrms[0:1, c * 256:(c + 1) * 256],
                                                       rr2[0:1, 256:512], AF.Exp, scale=-0.5),
                 reads=("rr2b",), writes=("rrms",), tag=f"Y{c} rexp")

    def outproj_phase(self, L):
        nc, P, sc, cst = self.nc, self.P, self.sc, self.cst
        pb = self.pb
        ytq, wob = L["ytq"], L["wob"]
        u, v, w, xtt, sqt = L["u"], L["v"], L["w"], L["xtt"], L["sqt"]
        rbr, bm1, br1, bm2, br2 = L["rbr"], L["bm1"], L["br1"], L["bm2"], L["br2"]
        row = L["row"]
        onesc, onesr = self.onesc, self.onesr
        m3R, r3R = self.m3R, self.r3R
        for tq in range(4):
            th, q2 = tq // 2, tq % 2
            t0 = tq * 512
            for (name, src, dst) in [("rbr", self.rrms[0:1, t0:t0 + 512], rbr),
                                     ("bm1", self.m1[0:1, th * THF + HLO + q2 * 512:th * THF + HLO + q2 * 512 + 512], bm1),
                                     ("br1", self.rstd1[0:1, th * THF + HLO + q2 * 512:th * THF + HLO + q2 * 512 + 512], br1)]:
                P.op("tensor", lambda e, src=src: e.matmul(pb[4][:, 0:512], onesr[0:1, 0:128], src,
                                                           start=True, stop=True),
                     reads=("rrms", "m1", "rstd1", "onesr"), writes=("pb4",), tag=f"op{tq} b_{name}")
                P.op("vector", lambda e, dst=dst: e.tensor_copy(dst[:], pb[4][:, 0:512]),
                     reads=("pb4",), writes=((name, "bm2") if name == "bm1" else
                                             ((name, "br2") if name == "br1" else (name,))),
                     tag=f"op{tq} be_{name}")
            for kt in range(64):
                P.dma("sync", ytq[:, kt * 512:(kt + 1) * 512],
                      sc["yp"][kt * 128:(kt + 1) * 128, t0:t0 + 512],
                      reads=("scyp",), writes=(f"ytq{kt}",), tag=f"op{tq} yld{kt}")
            for oc in range(32):
                slot = oc % 2
                P.dma("gpsimd", wob[:, slot * 8192:(slot + 1) * 8192], self.ap["w_out_t"][oc],
                      reads=(), writes=(f"wob{slot}",), tag=f"op{tq}.{oc} wld")
                for kt in range(64):
                    P.op("tensor", lambda e, kt=kt, slot=slot, oc=oc: e.matmul(
                        pb[oc % 2][:, 0:512], wob[:, slot * 8192 + kt * 128:slot * 8192 + (kt + 1) * 128],
                        ytq[:, kt * 512:(kt + 1) * 512],
                        start=(kt == 0), stop=(kt == 63), skip_group_check=True),
                        reads=(f"wob{slot}", f"ytq{kt}"),
                        writes=((f"pb{oc%2}",) if kt in (0, 63) else ()),
                        inc=(kt == 63), tag=f"op{tq}.{oc} mm{kt}")
                P.dma("sync", xtt[:], self.ap["xT"][oc * 128:(oc + 1) * 128, HLO + t0:HLO + t0 + 512],
                      writes=("xtt",), tag=f"op{tq}.{oc} xld")
                P.op("vector", lambda e, oc=oc: e.tensor_tensor(u[:], pb[oc % 2][:, 0:512], rbr[:],
                                                                ALU.mult),
                     reads=(f"pb{oc%2}", "rbr"), writes=("u",), tag=f"op{tq}.{oc} t1")
                P.op("vector", lambda e: e.tensor_tensor(v[:], xtt[:], bm1[:], ALU.subtract),
                     reads=("xtt", "bm1"), writes=("v",), tag=f"op{tq}.{oc} t2")
                P.op("vector", lambda e: e.tensor_tensor(w[:], v[:], br1[:], ALU.mult),
                     reads=("v", "br1"), writes=("w",), tag=f"op{tq}.{oc} t3")
                P.op("vector", lambda e: e.tensor_tensor(v[:], u[:], w[:], ALU.add),
                     reads=("u", "w"), writes=("v",), tag=f"op{tq}.{oc} mo")
                P.dma("sync", sc["mo"][oc * 128:(oc + 1) * 128, t0:t0 + 512], v[:],
                      reads=("v",), writes=("scmo",), tag=f"op{tq}.{oc} most")
                P.op("scalar", lambda e: e.activation(sqt[:], v[:], AF.Square),
                     reads=("v",), writes=("sqt",), tag=f"op{tq}.{oc} sq")
                P.op("tensor", lambda e, oc=oc: e.matmul(pb[2][0:1, 0:512], onesc[:], v[:],
                                                         start=(oc == 0), stop=(oc == 31),
                                                         skip_group_check=True),
                     reads=("v", "ones"), writes=(("pb2",) if oc in (0, 31) else ()),
                     inc=True, tag=f"op{tq}.{oc} smm")
                P.op("tensor", lambda e, oc=oc: e.matmul(pb[3][0:1, 0:512], onesc[:], sqt[:],
                                                         start=(oc == 0), stop=(oc == 31),
                                                         skip_group_check=True),
                     reads=("sqt", "ones"), writes=(("pb3",) if oc in (0, 31) else ()),
                     inc=True, tag=f"op{tq}.{oc} qmm")
            P.op("vector", lambda e: e.tensor_scalar(row[0:1, 0:512], pb[2][0:1, 0:512],
                                                     1.0 / DM, None, ALU.mult),
                 reads=("pb2",), writes=("rowm",), tag=f"op{tq} mean")
            P.op("vector", lambda e: e.tensor_scalar(row[0:1, 512:1024], pb[3][0:1, 0:512],
                                                     1.0 / DM, None, ALU.mult),
                 reads=("pb3",), writes=("rowq",), tag=f"op{tq} msq")
            P.op("vector", lambda e: e.tensor_tensor(row[0:1, 1024:1536], row[0:1, 0:512],
                                                     row[0:1, 0:512], ALU.mult),
                 reads=("rowm",), writes=("rowmm",), tag=f"op{tq} m2")
            P.op("vector", lambda e: e.tensor_tensor(row[0:1, 512:1024], row[0:1, 512:1024],
                                                     row[0:1, 1024:1536], ALU.subtract),
                 reads=("rowq", "rowmm"), writes=("rowq",), tag=f"op{tq} var")
            P.op("scalar", lambda e: e.activation(row[0:1, 1024:1536], row[0:1, 512:1024],
                                                  AF.Ln, bias=self.epsc[0:1, 0:1]),
                 reads=("rowq", "ceps"), writes=("rowmm",), tag=f"op{tq} ln")
            P.op("scalar", lambda e: e.activation(row[0:1, 512:1024], row[0:1, 1024:1536],
                                                  AF.Exp, scale=-0.5),
                 reads=("rowmm",), writes=("rowq",), tag=f"op{tq} rstd")
            P.op("tensor", lambda e: e.matmul(pb[4][:, 0:512], onesr[0:1, 0:128], row[0:1, 0:512],
                                              start=True, stop=True),
                 reads=("rowm", "onesr"), writes=("pb4",), tag=f"op{tq} bm2")
            P.op("vector", lambda e: e.tensor_copy(bm2[:], pb[4][:, 0:512]),
                 reads=("pb4",), writes=("bm2", "bm1"), tag=f"op{tq} bm2e")
            P.op("tensor", lambda e: e.matmul(pb[4][:, 0:512], onesr[0:1, 0:128], row[0:1, 512:1024],
                                              start=True, stop=True),
                 reads=("rowq", "onesr"), writes=("pb4",), tag=f"op{tq} br2")
            P.op("vector", lambda e: e.tensor_copy(br2[:], pb[4][:, 0:512]),
                 reads=("pb4",), writes=("br2", "br1"), tag=f"op{tq} br2e")
            for oc in range(32):
                P.dma("sync", u[:], sc["mo"][oc * 128:(oc + 1) * 128, t0:t0 + 512],
                      reads=("scmo",), writes=("u",), tag=f"pb{tq}.{oc} mold")
                P.dma("sync", xtt[:], self.ap["xT"][oc * 128:(oc + 1) * 128, HLO + t0:HLO + t0 + 512],
                      writes=("xtt",), tag=f"pb{tq}.{oc} xld")
                P.op("vector", lambda e: e.tensor_tensor(v[:], u[:], bm2[:], ALU.subtract),
                     reads=("u", "bm2"), writes=("v",), tag=f"pb{tq}.{oc} s1")
                P.op("vector", lambda e: e.tensor_tensor(w[:], v[:], br2[:], ALU.mult),
                     reads=("v", "br2"), writes=("w",), tag=f"pb{tq}.{oc} s2")
                P.op("vector", lambda e: e.tensor_tensor(v[:], w[:], xtt[:], ALU.add),
                     reads=("w", "xtt"), writes=("v",), tag=f"pb{tq}.{oc} x2")
                P.dma("sync", sc["x2"][oc * 128:(oc + 1) * 128, t0:t0 + 512], v[:],
                      reads=("v",), writes=("scx2",), tag=f"pb{tq}.{oc} x2st")
                P.op("scalar", lambda e: e.activation(sqt[:], v[:], AF.Square),
                     reads=("v",), writes=("sqt",), tag=f"pb{tq}.{oc} sq")
                P.op("tensor", lambda e, oc=oc: e.matmul(pb[2][0:1, 0:512], onesc[:], v[:],
                                                         start=(oc == 0), stop=(oc == 31),
                                                         skip_group_check=True),
                     reads=("v", "ones"), writes=(("pb2",) if oc in (0, 31) else ()),
                     inc=True, tag=f"pb{tq}.{oc} smm")
                P.op("tensor", lambda e, oc=oc: e.matmul(pb[3][0:1, 0:512], onesc[:], sqt[:],
                                                         start=(oc == 0), stop=(oc == 31),
                                                         skip_group_check=True),
                     reads=("sqt", "ones"), writes=(("pb3",) if oc in (0, 31) else ()),
                     inc=True, tag=f"pb{tq}.{oc} qmm")
            P.op("vector", lambda e, t0=t0: e.tensor_scalar(m3R[0:1, t0:t0 + 512], pb[2][0:1, 0:512],
                                                            1.0 / DM, None, ALU.mult),
                 reads=("pb2",), writes=("m3R",), tag=f"pb{tq} mean")
            P.op("vector", lambda e: e.tensor_scalar(row[0:1, 512:1024], pb[3][0:1, 0:512],
                                                     1.0 / DM, None, ALU.mult),
                 reads=("pb3",), writes=("rowq",), tag=f"pb{tq} msq")
            P.op("vector", lambda e, t0=t0: e.tensor_tensor(row[0:1, 1024:1536], m3R[0:1, t0:t0 + 512],
                                                            m3R[0:1, t0:t0 + 512], ALU.mult),
                 reads=("m3R",), writes=("rowmm",), tag=f"pb{tq} m2")
            P.op("vector", lambda e: e.tensor_tensor(row[0:1, 512:1024], row[0:1, 512:1024],
                                                     row[0:1, 1024:1536], ALU.subtract),
                 reads=("rowq", "rowmm"), writes=("rowq",), tag=f"pb{tq} var")
            P.op("scalar", lambda e: e.activation(row[0:1, 1024:1536], row[0:1, 512:1024],
                                                  AF.Ln, bias=self.epsc[0:1, 0:1]),
                 reads=("rowq", "ceps"), writes=("rowmm",), tag=f"pb{tq} ln")
            P.op("scalar", lambda e, t0=t0: e.activation(r3R[0:1, t0:t0 + 512], row[0:1, 1024:1536],
                                                         AF.Exp, scale=-0.5),
                 reads=("rowmm",), writes=("r3R",), tag=f"pb{tq} rstd")

    def ffn_phase(self, L):
        nc, P, sc, cst = self.nc, self.P, self.sc, self.cst
        pb = self.pb
        x2l, wbb = L["x2l"], L["wbb"]
        u, v, w, sqt = L["u2"], L["v2"], L["w2"], L["sqt2"]
        bma, bmb = L["bma"], L["bmb"]
        lg, ltm, comb, cfm, ce, hgb = L["lg"], L["ltm"], L["comb"], L["cfm"], L["ce"], L["hgb"]
        row = None
        onesc, onesr = self.onesc, self.onesr
        m3R, r3R = self.m3R, self.r3R
        idf = cst["identity"]
        P.dma("gpsimd", self.wrT[:], self.ap["w_router_t"], writes=("c_wrT",), tag="ld_wrT")
        P.dma("gpsimd", self.aallT[:], self.ap["a_all_t"], writes=("c_aallT",), tag="ld_aallT")
        P.dma("gpsimd", self.bflat[:], self.ap["b_flat"], writes=("c_bflat",), tag="ld_bflat")
        for th in range(2):
            t0 = th * 1024
            for q2 in range(2):
                tq0 = t0 + q2 * 512
                P.op("tensor", lambda e, tq0=tq0: e.matmul(pb[4][:, 0:512], onesr[0:1, 0:128],
                                                           m3R[0:1, tq0:tq0 + 512], start=True, stop=True),
                     reads=("m3R", "onesr"), writes=("pb4",), tag=f"f{th}.{q2} bm3")
                P.op("vector", lambda e: e.tensor_copy(bma[:], pb[4][:, 0:512]),
                     reads=("pb4",), writes=("bma",), tag=f"f{th}.{q2} bm3e")
                P.op("tensor", lambda e, tq0=tq0: e.matmul(pb[4][:, 0:512], onesr[0:1, 0:128],
                                                           r3R[0:1, tq0:tq0 + 512], start=True, stop=True),
                     reads=("r3R", "onesr"), writes=("pb4",), tag=f"f{th}.{q2} br3")
                P.op("vector", lambda e: e.tensor_copy(bmb[:], pb[4][:, 0:512]),
                     reads=("pb4",), writes=("bmb",), tag=f"f{th}.{q2} br3e")
                for oc in range(32):
                    P.dma("sync", u[:], sc["x2"][oc * 128:(oc + 1) * 128, tq0:tq0 + 512],
                          reads=("scx2",), writes=("u2",), tag=f"f{th}.{q2}.{oc} x2ld")
                    P.op("vector", lambda e: e.tensor_tensor(v[:], u[:], bma[:], ALU.subtract),
                         reads=("u2", "bma"), writes=("v2",), tag=f"f{th}.{q2}.{oc} s")
                    P.op("vector", lambda e, oc=oc, q2=q2: e.tensor_tensor(
                        x2l[:, oc * 1024 + q2 * 512:oc * 1024 + (q2 + 1) * 512], v[:], bmb[:], ALU.mult),
                        reads=("v2", "bmb"), writes=(f"x2l{oc}",), tag=f"f{th}.{q2}.{oc} m")
            for nt in range(2):
                for kt in range(NKT):
                    P.op("tensor", lambda e, nt=nt, kt=kt: e.matmul(
                        pb[5 + nt][0:8, 0:512], self.wrT[:, kt * 8:(kt + 1) * 8],
                        x2l[:, kt * 1024 + nt * 512:kt * 1024 + (nt + 1) * 512],
                        start=(kt == 0), stop=(kt == NKT - 1), skip_group_check=True),
                        reads=("c_wrT", f"x2l{kt}"),
                        writes=((f"pb{5+nt}",) if kt in (0, NKT - 1) else ()),
                        inc=True, tag=f"f{th} lg{nt}.{kt}")
                P.op("vector", lambda e, nt=nt: e.tensor_copy(lg[0:8, nt * 512:(nt + 1) * 512],
                                                              pb[5 + nt][0:8, 0:512]),
                     reads=(f"pb{5+nt}",), writes=("lg",), tag=f"f{th} lge{nt}")
            for tt in range(8):
                P.op("tensor", lambda e, tt=tt: e.transpose(pb[4][0:128, 0:8],
                                                            lg[0:8, tt * 128:(tt + 1) * 128],
                                                            idf[0:8, 0:8]),
                     reads=("lg", "c_identity"), writes=("pb4",), tag=f"f{th}.{tt} ltr")
                P.op("vector", lambda e: e.tensor_copy(ltm[:, 0:8], pb[4][0:128, 0:8]),
                     reads=("pb4",), writes=("ltm",), tag=f"f{th}.{tt} ltre")
                P.op("vector", lambda e: e.reduce_max(ltm[:, 8:9], ltm[:, 0:8], axis=mybir.AxisListType.X),
                     reads=("ltm",), writes=("ltm1",), tag=f"f{th}.{tt} max1")
                P.op("vector", lambda e: e.tensor_scalar(comb[:, 0:8], ltm[:, 0:8], ltm[:, 8:9],
                                                         None, ALU.is_ge),
                     reads=("ltm", "ltm1"), writes=("mask1", "combf"), tag=f"f{th}.{tt} msk1")
                P.op("vector", lambda e: e.scalar_tensor_tensor(comb[:, 8:16], comb[:, 0:8], -1e30,
                                                                ltm[:, 0:8], ALU.mult, ALU.add),
                     reads=("mask1", "ltm"), writes=("l2b",), tag=f"f{th}.{tt} l2")
                P.op("vector", lambda e: e.reduce_max(ltm[:, 9:10], comb[:, 8:16],
                                                      axis=mybir.AxisListType.X),
                     reads=("l2b",), writes=("ltm2",), tag=f"f{th}.{tt} max2")
                P.op("vector", lambda e: e.tensor_scalar(comb[:, 16:24], comb[:, 8:16], ltm[:, 9:10],
                                                         None, ALU.is_ge),
                     reads=("l2b", "ltm2"), writes=("mask2",), tag=f"f{th}.{tt} msk2")
                P.op("vector", lambda e: e.tensor_tensor(ltm[:, 10:11], ltm[:, 8:9], ltm[:, 9:10],
                                                         ALU.subtract),
                     reads=("ltm1", "ltm2"), writes=("dgap",), tag=f"f{th}.{tt} d")
                P.op("scalar", lambda e: e.activation(ltm[:, 11:12], ltm[:, 10:11], AF.Sigmoid),
                     reads=("dgap",), writes=("p1",), tag=f"f{th}.{tt} sig")
                P.op("vector", lambda e: e.tensor_scalar(ltm[:, 12:13], ltm[:, 11:12], 2.0, None,
                                                         ALU.mult),
                     reads=("p1",), writes=("p1s",), tag=f"f{th}.{tt} p1s")
                P.op("vector", lambda e: e.tensor_scalar(ltm[:, 13:14], ltm[:, 11:12], -2.0, 2.0,
                                                         ALU.mult, ALU.add),
                     reads=("p1",), writes=("p2s",), tag=f"f{th}.{tt} p2s")
                P.op("vector", lambda e: e.tensor_scalar(comb[:, 24:32], comb[:, 0:8], ltm[:, 12:13],
                                                         None, ALU.mult),
                     reads=("mask1", "p1s"), writes=("c1",), tag=f"f{th}.{tt} c1")
                P.op("vector", lambda e: e.tensor_scalar(comb[:, 8:16], comb[:, 16:24], ltm[:, 13:14],
                                                         None, ALU.mult),
                     reads=("mask2", "p2s"), writes=("c2",), tag=f"f{th}.{tt} c2")
                P.op("vector", lambda e: e.tensor_tensor(comb[:, 0:8], comb[:, 24:32], comb[:, 8:16],
                                                         ALU.add),
                     reads=("c1", "c2"), writes=("combf",), tag=f"f{th}.{tt} cmb")
                P.op("tensor", lambda e: e.transpose(pb[4][0:8, 0:128], comb[:, 0:8], idf[:]),
                     reads=("combf", "c_identity"), writes=("pb4",), tag=f"f{th}.{tt} ctr")
                P.op("vector", lambda e, tt=tt: e.tensor_copy(cfm[0:8, tt * 128:(tt + 1) * 128],
                                                              pb[4][0:8, 0:128]),
                     reads=("pb4",), writes=("cfm",), tag=f"f{th}.{tt} ctre")
            for nt in range(2):
                P.op("tensor", lambda e, nt=nt: e.matmul(pb[4][:, 0:512], cst["e8"][0:8, :],
                                                         cfm[0:8, nt * 512:(nt + 1) * 512],
                                                         start=True, stop=True),
                     reads=("cfm", "c_e8"), writes=("pb4",), tag=f"f{th} ex{nt}")
                P.op("vector", lambda e, nt=nt: e.tensor_copy(ce[:, nt * 512:(nt + 1) * 512],
                                                              pb[4][:, 0:512]),
                     reads=("pb4",), writes=("ce",), tag=f"f{th} exe{nt}")
            for nt in range(2):
                for kt in range(NKT):
                    P.op("tensor", lambda e, nt=nt, kt=kt: e.matmul(
                        pb[5][:, 0:512], self.aallT[:, kt * 128:(kt + 1) * 128],
                        x2l[:, kt * 1024 + nt * 512:kt * 1024 + (nt + 1) * 512],
                        start=(kt == 0), stop=(kt == NKT - 1), skip_group_check=True),
                        reads=("c_aallT", f"x2l{kt}"),
                        writes=(("pb5",) if kt in (0, NKT - 1) else ()),
                        inc=True, tag=f"f{th} h{nt}.{kt}")
                P.op("vector", lambda e, nt=nt: e.tensor_tensor(hgb[:, nt * 512:(nt + 1) * 512],
                                                                pb[5][:, 0:512],
                                                                ce[:, nt * 512:(nt + 1) * 512], ALU.mult),
                     reads=("pb5", "ce"), writes=("hgb",), tag=f"f{th} hg{nt}")
            for oc in range(32):
                slot = oc % 2
                P.dma("gpsimd", wbb[:, slot * 4096:(slot + 1) * 4096], self.ap["w_base_t"][oc],
                      reads=(), writes=(f"wbb{slot}",), tag=f"f{th}.{oc} wld")
                for nt in range(2):
                    for kt in range(NKT):
                        P.op("tensor", lambda e, nt=nt, kt=kt, slot=slot, oc=oc: e.matmul(
                            pb[oc % 2][:, 0:512],
                            wbb[:, slot * 4096 + kt * 128:slot * 4096 + (kt + 1) * 128],
                            x2l[:, kt * 1024 + nt * 512:kt * 1024 + (nt + 1) * 512],
                            start=(kt == 0), stop=False, skip_group_check=True),
                            reads=(f"wbb{slot}", f"x2l{kt}"),
                            writes=((f"pb{oc%2}",) if kt == 0 else ()),
                            inc=False, tag=f"f{th}.{oc} bmm{nt}.{kt}")
                    P.op("tensor", lambda e, nt=nt, oc=oc: e.matmul(
                        pb[oc % 2][:, 0:512], self.bflat[:, oc * 128:(oc + 1) * 128],
                        hgb[:, nt * 512:(nt + 1) * 512],
                        start=False, stop=True, skip_group_check=True),
                        reads=("c_bflat", "hgb"), writes=(f"pb{oc%2}",),
                        inc=True, tag=f"f{th}.{oc} lora{nt}")
                    P.op("vector", lambda e, nt=nt, oc=oc: e.tensor_scalar(
                        u[:], pb[oc % 2][:, 0:512], cst["b_base32"][:, oc:oc + 1], None, ALU.add),
                        reads=(f"pb{oc%2}", "c_b_base32"), writes=("u2",), tag=f"f{th}.{oc} bias{nt}")
                    P.dma("sync", sc["ffn"][oc * 128:(oc + 1) * 128, t0 + nt * 512:t0 + (nt + 1) * 512],
                          u[:], reads=("u2",), writes=("scffn",), tag=f"f{th}.{oc} fst{nt}")
                    P.op("scalar", lambda e: e.activation(sqt[:], u[:], AF.Square),
                         reads=("u2",), writes=("sqt2",), tag=f"f{th}.{oc} sq{nt}")
                    sbk = 2 if nt == 0 else 6
                    qbk = 3 if nt == 0 else 4
                    P.op("tensor", lambda e, oc=oc, sbk=sbk: e.matmul(
                        pb[sbk][0:1, 0:512], onesc[:], u[:],
                        start=(oc == 0), stop=(oc == 31), skip_group_check=True),
                        reads=("u2", "ones"), writes=((f"pb{sbk}",) if oc in (0, 31) else ()),
                        inc=True, tag=f"f{th}.{oc} smm{nt}")
                    P.op("tensor", lambda e, oc=oc, qbk=qbk: e.matmul(
                        pb[qbk][0:1, 0:512], onesc[:], sqt[:],
                        start=(oc == 0), stop=(oc == 31), skip_group_check=True),
                        reads=("sqt2", "ones"), writes=((f"pb{qbk}",) if oc in (0, 31) else ()),
                        inc=True, tag=f"f{th}.{oc} qmm{nt}")
            rowm = self.m3R[0:1, 0:1024]
            rowq = self.r3R[0:1, 0:1024]
            rowmm = self.m3R[0:1, 1024:2048]
            P.op("vector", lambda e, rowm=rowm: e.tensor_scalar(rowm[0:1, 0:512], pb[2][0:1, 0:512],
                                                                1.0 / DM, None, ALU.mult),
                 reads=("pb2",), writes=("m3R",), tag=f"f{th} mean0")
            P.op("vector", lambda e, rowm=rowm: e.tensor_scalar(rowm[0:1, 512:1024], pb[6][0:1, 0:512],
                                                                1.0 / DM, None, ALU.mult),
                 reads=("pb6",), writes=("m3R",), tag=f"f{th} mean1")
            P.op("vector", lambda e, rowq=rowq: e.tensor_scalar(rowq[0:1, 0:512], pb[3][0:1, 0:512],
                                                                1.0 / DM, None, ALU.mult),
                 reads=("pb3",), writes=("r3R",), tag=f"f{th} msq0")
            P.op("vector", lambda e, rowq=rowq: e.tensor_scalar(rowq[0:1, 512:1024], pb[4][0:1, 0:512],
                                                                1.0 / DM, None, ALU.mult),
                 reads=("pb4",), writes=("r3R",), tag=f"f{th} msq1")
            P.op("vector", lambda e, rowm=rowm, rowmm=rowmm: e.tensor_tensor(
                rowmm, rowm, rowm, ALU.mult),
                 reads=("m3R",), writes=("m3R",), tag=f"f{th} m2")
            P.op("vector", lambda e, rowq=rowq, rowmm=rowmm: e.tensor_tensor(
                rowq, rowq, rowmm, ALU.subtract),
                 reads=("r3R", "m3R"), writes=("r3R",), tag=f"f{th} var")
            P.op("scalar", lambda e, rowq=rowq, rowmm=rowmm: e.activation(
                rowmm, rowq, AF.Ln, bias=self.epsc[0:1, 0:1]),
                 reads=("r3R", "ceps"), writes=("m3R",), tag=f"f{th} ln")
            P.op("scalar", lambda e, rowq=rowq, rowmm=rowmm: e.activation(
                rowq, rowmm, AF.Exp, scale=-0.5),
                 reads=("m3R",), writes=("r3R",), tag=f"f{th} rstd")
            for q2 in range(2):
                tq0 = t0 + q2 * 512
                P.op("tensor", lambda e, q2=q2: e.matmul(pb[4][:, 0:512], onesr[0:1, 0:128],
                                                         self.m3R[0:1, q2 * 512:(q2 + 1) * 512],
                                                         start=True, stop=True),
                     reads=("m3R", "onesr"), writes=("pb4",), tag=f"f{th}.{q2} bm4")
                P.op("vector", lambda e: e.tensor_copy(bma[:], pb[4][:, 0:512]),
                     reads=("pb4",), writes=("bma",), tag=f"f{th}.{q2} bm4e")
                P.op("tensor", lambda e, q2=q2: e.matmul(pb[4][:, 0:512], onesr[0:1, 0:128],
                                                         self.r3R[0:1, q2 * 512:(q2 + 1) * 512],
                                                         start=True, stop=True),
                     reads=("r3R", "onesr"), writes=("pb4",), tag=f"f{th}.{q2} br4")
                P.op("vector", lambda e: e.tensor_copy(bmb[:], pb[4][:, 0:512]),
                     reads=("pb4",), writes=("bmb",), tag=f"f{th}.{q2} br4e")
                for oc in range(32):
                    P.dma("sync", u[:], sc["ffn"][oc * 128:(oc + 1) * 128, tq0:tq0 + 512],
                          reads=("scffn",), writes=("u2",), tag=f"o{th}.{q2}.{oc} fld")
                    P.dma("sync", w[:], sc["x2"][oc * 128:(oc + 1) * 128, tq0:tq0 + 512],
                          reads=("scx2",), writes=("w2",), tag=f"o{th}.{q2}.{oc} x2ld")
                    P.op("vector", lambda e: e.tensor_tensor(v[:], u[:], bma[:], ALU.subtract),
                         reads=("u2", "bma"), writes=("v2",), tag=f"o{th}.{q2}.{oc} s1")
                    P.op("vector", lambda e: e.tensor_tensor(u[:], v[:], bmb[:], ALU.mult),
                         reads=("v2", "bmb"), writes=("u2",), tag=f"o{th}.{q2}.{oc} s2")
                    P.op("vector", lambda e: e.tensor_tensor(v[:], u[:], w[:], ALU.add),
                         reads=("u2", "w2"), writes=("v2",), tag=f"o{th}.{q2}.{oc} out")
                    P.dma("sync", self.out_t[oc * 128:(oc + 1) * 128, tq0:tq0 + 512], v[:],
                          reads=("v2",), writes=("outT",), tag=f"o{th}.{q2}.{oc} ost")

    def sub2_ssd(self):
        nc, P = self.nc, self.P
        P.barrier("b_ssd")
        from contextlib import ExitStack
        if True:
            with ExitStack() as st:
                sbs = lambda n, shp, dt: st.enter_context(nc.sbuf_tensor(n, shp, dt))
                S0 = sbs("S0", [128, DI], F32)
                with nc.sbuf_tensor("stg", [128, DI], F32) as stg:
                    P.op("vector", lambda e: e.memset(S0[:], 0.0), writes=("S0",), tag="S0 z")
                    for r in range(8):
                        P.dma("sync", stg[:], self.cc_out[r * 128:(r + 1) * 128, :],
                              reads=("ccout",), writes=("stg",), tag=f"S0 ld{r}")
                        P.op("vector", lambda e, r=r: e.scalar_tensor_tensor(
                            S0[:], stg[:], self.cst["msel"][:, r:r + 1], S0[:], ALU.mult, ALU.add),
                            reads=("stg", "c_msel", "S0"), writes=("S0",), tag=f"S0 acc{r}")
                    P.barrier("b_stg")
                    P.emit()
                L = dict(S0=S0,
                         sstg=sbs("sstg", [128, 1024], F32),
                         SfG=sbs("SfG", [128, 2048], F32),
                         Sfb=sbs("Sfb", [128, 2048], BF16),
                         gB=sbs("gB", [128, 512], BF16), gC2=sbs("gC2", [128, 1024], BF16),
                         gxdt2=sbs("gxdt2", [128, 4096], BF16),
                         gxs2=sbs("gxs2", [128, 4096], BF16),
                         gz2=sbs("gz2", [128, 4096], BF16),
                         k2a=sbs("k2a", [2, 4096], F32), k2b=sbs("k2b", [2, 4096], F32),
                         odf=self.r3R[0:1, 0:2048].bitcast(BF16),
                         l3=sbs("l3", [128, 768], F32), m3=sbs("m3", [128, 1536], BF16),
                         Gm2=sbs("Gm2", [128, 768], F32),
                         cod=sbs("cod", [128, 1024], BF16),
                         yst=sbs("yst", [128, 256], F32),
                         ygb=sbs("ygb", [128, 256], F32), ypb=sbs("ypb", [128, 512], BF16),
                         ysq=sbs("ysq", [128, 512], F32), acc=sbs("acc", [128, 256], F32),
                         rr2=self.m3R[0:1, 0:512])
                P.op("vector", lambda e: e.memset(L["k2a"][0:1, :], 1.0), writes=("k2ac",), tag="k2a1")
                P.op("vector", lambda e: e.memset(L["k2b"][0:1, :], -1.0), writes=("k2b0",), tag="k2bn")
                P.dma("sync", L["k2b"][1:2, :], L["k2b"][0:1, :], reads=("k2b0",),
                      writes=("k2bc",), tag="k2bc cp")
                if STOP_AFTER != "ssd":
                    pass
                self.ssd_phase(L)
                P.emit()

    def sub2_rest(self):
        nc, P = self.nc, self.P
        P.barrier("b_op")
        from contextlib import ExitStack
        if True:
            with ExitStack() as st:
                sbs = lambda n, shp, dt: st.enter_context(nc.sbuf_tensor(n, shp, dt))
                L = dict(ytq=sbs("ytq", [128, 64 * 512], BF16),
                         wob=sbs("wob", [128, 2 * 8192], BF16),
                         u=sbs("u", [128, 512], F32), v=sbs("v", [128, 512], F32),
                         w=sbs("w", [128, 512], F32), xtt=sbs("xtt", [128, 512], F32),
                         sqt=sbs("sqt", [128, 512], F32),
                         rbr=sbs("rbr", [128, 512], F32), bm1=sbs("bm1", [128, 512], F32),
                         br1=sbs("br1", [128, 512], F32), row=sbs("rowb", [1, 1536], F32))
                L["bm2"] = L["bm1"]
                L["br2"] = L["br1"]
                self.outproj_phase(L)
                P.emit()
            if STOP_AFTER == "op":
                return
            P.barrier("b_ffn")
            with ExitStack() as st:
                sbs = lambda n, shp, dt: st.enter_context(nc.sbuf_tensor(n, shp, dt))
                self.wrT = sbs("c_wrT", [128, 256], BF16)
                self.aallT = sbs("c_aallT", [128, 4096], BF16)
                self.bflat = sbs("c_bflat", [128, 4096], BF16)
                L = dict(x2l=sbs("x2l", [128, 32 * 1024], BF16),
                         wbb=sbs("wbb", [128, 2 * 4096], BF16),
                         u2=sbs("u2", [128, 512], F32), v2=sbs("v2", [128, 512], F32),
                         w2=sbs("w2", [128, 512], F32), sqt2=sbs("sqt2", [128, 512], F32),
                         bma=sbs("bma", [128, 512], F32), bmb=sbs("bmb", [128, 512], F32),
                         lg=sbs("lg", [8, 1024], F32), ltm=sbs("ltm", [128, 16], F32),
                         comb=sbs("comb", [128, 32], F32), cfm=sbs("cfm", [8, 1024], F32),
                         ce=sbs("ce", [128, 1024], F32), hgb=sbs("hgb", [128, 1024], BF16))
                self.ffn_phase(L)
                P.emit()


def kernel(**inputs):
    sh, per = _stage_host(inputs)
    nc = bass.Bass("TRN2", target_bir_lowering=False)
    mk = MK(nc, list(DEBUG_TAPS))
    for name in list(DEBUG_TAPS):
        mk.tap_sc(name)
    mk.build()
    in_maps = []
    for c in range(N_CORES):
        m = dict(sh)
        m.update(per[c])
        in_maps.append(m)
    res = run_bass_kernel_spmd(nc, in_maps, core_ids=list(range(N_CORES)))
    global LAST_TAPS
    LAST_TAPS = [{k: v for k, v in r.items() if k.startswith("tap_")} for r in res.results]
    outs = [r["out_t"] for r in res.results]
    full = np.zeros((4, 4096, 4096), np.float32)
    for c in range(N_CORES):
        s, h = c // 2, c % 2
        full[s, h * T:(h + 1) * T, :] = outs[c].T
    return full
